# version 15
# speedup vs baseline: 1.2344x; 1.0799x over previous
import numpy as np
from contextlib import ExitStack

import concourse.bass as bass
import concourse.mybir as mybir
from concourse.bass_utils import run_bass_kernel_spmd

F32 = mybir.dt.float32
BF16 = mybir.dt.bfloat16
U32 = mybir.dt.uint32
AF = mybir.ActivationFunctionType
ALU = mybir.AluOpType
AXX = mybir.AxisListType.X
IOX = bass.IndirectOffsetOnAxis

P = 128
NCH = 8
D = 1024
T = 2048
NB = 4
TA = 512
TB = 512
E = 8
FF = 2048
NF = 16
EPS = 1e-5
LCL = float(np.log(np.float32(1e-7)))
RS2 = float(1.0 / np.sqrt(2.0))


class Sem:
    def __init__(self, ctx, nc, name):
        self.h = ctx.enter_context(nc.semaphore(name))
        self.n = 0

    def inc(self, instr, k=1):
        instr.then_inc(self.h, k)
        self.n += k
        return self.n


def build_nc(nb=NB, nta=T // TA, do_b=True, dbg=False):
    ntb = T // TB
    nc = bass.Bass("TRN2", target_bir_lowering=False, debug=False)

    from concourse.hw_specs import get_activation_tables
    _tabs = list(get_activation_tables(nc.m.arch).values())
    SET_LE = next(i for i, f in enumerate(_tabs)
                  if AF.Ln in f and AF.Exp in f)
    SET_SG = next(i for i, f in enumerate(_tabs)
                  if AF.Sigmoid in f and AF.Erf in f)
    SET_GE = next(i for i, f in enumerate(_tabs) if AF.Gelu in f)

    def ld_act(eng, sid):
        eng.add_instruction(mybir.InstLoadActFuncSet(
            name=nc.get_next_instruction_name(), act_func_set_id=sid,
            ins=[], outs=[]))

    xT = nc.dram_tensor("xT", [NB, D, T], F32, kind="ExternalInput").ap()
    W1 = nc.dram_tensor("W1", [E, D, FF], F32, kind="ExternalInput").ap()
    W2 = nc.dram_tensor("W2", [E, FF, D], F32, kind="ExternalInput").ap()
    Wo = nc.dram_tensor("Wo", [D, D], F32, kind="ExternalInput").ap()
    Wp1 = nc.dram_tensor("Wp1", [D, P], F32, kind="ExternalInput").ap()
    Wp2 = nc.dram_tensor("Wp2", [P, E], F32, kind="ExternalInput").ap()
    b1d = nc.dram_tensor("b1d", [E, FF], F32, kind="ExternalInput").ap()
    b2d = nc.dram_tensor("b2d", [E, D], F32, kind="ExternalInput").ap()
    igw = nc.dram_tensor("igw", [E, P, P], F32, kind="ExternalInput").ap()
    agw = nc.dram_tensor("agw", [E, P, P], F32, kind="ExternalInput").ap()
    igb = nc.dram_tensor("igb", [NCH, P], F32, kind="ExternalInput").ap()
    agb = nc.dram_tensor("agb", [NCH, P], F32, kind="ExternalInput").ap()
    apar = nc.dram_tensor("apar", [NCH, P], F32, kind="ExternalInput").ap()
    bp1d = nc.dram_tensor("bp1d", [P, 1], F32, kind="ExternalInput").ap()
    bp2d = nc.dram_tensor("bp2d", [1, E], F32, kind="ExternalInput").ap()
    bod = nc.dram_tensor("bod", [NCH, P], F32, kind="ExternalInput").ap()

    oT = nc.dram_tensor("oT", [NB, D, T], F32, kind="ExternalOutput").ap()
    x2sp = nc.dram_tensor("dbg_x2" if dbg else "x2sp", [NB, D, T], F32,
                          kind="ExternalOutput" if dbg else "Internal").ap()
    r3sp = nc.dram_tensor("dbg_r3" if dbg else "r3sp", [NB, 2, T], F32,
                          kind="ExternalOutput" if dbg else "Internal").ap()
    if dbg:
        dbg_y = nc.dram_tensor("dbg_y", [NB, D, T], F32,
                               kind="ExternalOutput").ap()
        dbg_rt = nc.dram_tensor("dbg_rt", [NB, 64], F32,
                                kind="ExternalOutput").ap()

    W1f = W1.rearrange("e k f -> (e k) f")
    W2f = W2.rearrange("e f h -> (e f) h")
    r3f = r3sp.rearrange("b k t -> (b k) t")

    with ExitStack() as top:
        def sb(name, shape, dtype):
            return top.enter_context(nc.sbuf_tensor(name, shape, dtype))

        def sem(name):
            return Sem(top, nc, name)

        ones128 = sb("ones128", [P, P], BF16)
        onesrow = sb("onesrow", [1, P], BF16)
        igb_sb = sb("igb_sb", [P, NCH], F32)
        agb_sb = sb("agb_sb", [P, NCH], F32)
        m8c = sb("m8c", [P, NCH], F32)
        wp2_sb = sb("wp2_sb", [P, E], BF16)
        bp1_sb = sb("bp1_sb", [P, 1], F32)
        bp1s_sb = sb("bp1s_sb", [P, 1], F32)
        bp2_sb = sb("bp2_sb", [1, E], F32)
        iotaW1 = sb("iotaW1", [P, NCH], F32)
        iotaW2 = sb("iotaW2", [P, NF], F32)
        iotaE1 = sb("iotaE1", [1, E], F32)
        iotaE2 = sb("iotaE2", [1, E], F32)
        zrow = sb("zrow", [1, E], F32)
        ones8t = sb("ones8t", [E, TB], BF16)
        ones1t = sb("ones1t", [1, TB], BF16)
        idxR3 = sb("idxR3", [P, 2 * NB], U32)
        Hcar = sb("Hcar", [P, NB * 2 * NCH], F32)
        ycar = sb("ycar", [P, NB * NCH], F32)
        x3lsb = sb("x3lsb", [P, NCH], F32)
        x2lsb = sb("x2lsb", [P, NCH], F32)
        w1h0 = sb("w1h0", [P, (NCH // 2) * FF], BF16)
        idnb = sb("idnb", [P, P], BF16)
        idnf = sb("idnf", [P, P], F32)
        x3lbf = sb("x3lbf", [P, NCH], BF16)
        gvec = sb("gvec", [P, 1], F32)
        evec = sb("evec", [P, 1], F32)
        wvec = sb("wvec", [P, 1], F32)
        bar1 = sb("bar1", [P, 1], F32)
        epsc = sb("epsc", [P, 1], F32)
        bar2 = sb("bar2", [P, 1], F32)
        u1bf = sb("u1bf", [P, 1], BF16)
        lgrow = sb("lgrow", [1, E], F32)
        mxr = sb("mxr", [1, 1], F32)
        ohrow = sb("ohrow", [1, E], F32)
        incl = sb("incl", [1, E], F32)
        fer = sb("fer", [1, E], F32)
        t8a = sb("t8a", [1, E], F32)
        t8b = sb("t8b", [1, E], F32)
        s1k = sb("s1k", [1, 1], F32)
        s2k = sb("s2k", [1, 1], F32)
        s1kb = sb("s1kb", [1, 1], BF16)
        s2kb = sb("s2kb", [1, 1], BF16)
        oh32 = sb("oh32", [32, 32], F32)
        ohT = sb("ohT", [32, 32], F32)
        selc = sb("selc", [P, 1], F32)
        idx1f = sb("idx1f", [P, NCH], F32)
        idx2f = sb("idx2f", [P, NF], F32)
        idx1u = sb("idx1u", [P, NCH * NB], U32)
        idx2u = sb("idx2u", [P, NF * NB], U32)
        ohcols = sb("ohcols", [E, NB], F32)

        LDP = sem("LDP")
        GPP = sem("GPP")
        SPR = sem("SPR")
        VPR = sem("VPR")
        LDX = sem("LDX")
        GXB = sem("GXB")
        S_SQ = sem("S_SQ")
        T_ST1 = sem("T_ST1")
        V_ST1 = sem("V_ST1")
        S_RI1 = sem("S_RI1")
        V_X1 = sem("V_X1")
        G_X1 = sem("G_X1")
        GX1B = sem("GX1B")
        T_G = sem("T_G")
        S_SIG = sem("S_SIG")
        V_LA = sem("V_LA")
        V_HS = sem("V_HS")
        V_D = sem("V_D")
        S_EXP = sem("S_EXP")
        V_M2 = sem("V_M2")
        S_MUL = sem("S_MUL")
        V_YS = sem("V_YS")
        V_CAR = sem("V_CAR")
        G_CAR = sem("G_CAR")
        GYB = sem("GYB")
        S_SQY = sem("S_SQY")
        T_ST2 = sem("T_ST2")
        V_ST2 = sem("V_ST2")
        S_RI2 = sem("S_RI2")
        V_AP2 = sem("V_AP2")
        G_MN2 = sem("G_MN2")
        V_X2 = sem("V_X2")
        SPL = sem("SPL")
        S_SQX2 = sem("S_SQX2")
        T_ST3 = sem("T_ST3")
        V_ST3 = sem("V_ST3")
        S_RI3 = sem("S_RI3")
        V_MN3 = sem("V_MN3")
        SPL3 = sem("SPL3")
        V_RT = sem("V_RT")
        GX3L = sem("GX3L")
        T_RT = sem("T_RT")
        S_RT = sem("S_RT")
        RTD = sem("RTD")
        BARR = sem("BARR")
        DBGY = sem("DBGY")
        GWLD = sem("GWLD")
        V_OHR = sem("V_OHR")
        ACC = sem("ACC")
        GW1 = sem("GW1")
        GW2 = sem("GW2")
        LB = sem("LB")
        GR3 = sem("GR3")
        V_X3M = sem("V_X3M")
        G_X3 = sem("G_X3")
        GX3B = sem("GX3B")
        T_H = sem("T_H")
        S_H = sem("S_H")
        T_R = sem("T_R")
        S_R = sem("S_R")
        T_O = sem("T_O")
        V_FF = sem("V_FF")
        G_FF = sem("G_FF")
        GFB = sem("GFB")
        S_SQ4 = sem("S_SQ4")
        T_S4 = sem("T_S4")
        V_ST4 = sem("V_ST4")
        S_RI4 = sem("S_RI4")
        V_FM = sem("V_FM")
        G_FM = sem("G_FM")
        V_FIN = sem("V_FIN")
        OUT = sem("OUT")

        block = top.enter_context(nc.Block())
        mst = {}

        def pro_sync(sync):
            for tsb, src in ((igb_sb, igb), (agb_sb, agb), (m8c, apar)):
                LDP.inc(sync.dma_start(out=tsb[:],
                                       in_=src.rearrange("c p -> p c"),
                                       allow_slow_non_contiguous=True), 16)
            LDP.inc(sync.dma_start(out=bp1_sb[:], in_=bp1d[:]), 16)
            LDP.inc(sync.dma_start(out=bp2_sb[:], in_=bp2d[:]), 16)
        block.sync(pro_sync)
        ldp_all = LDP.n

        def pro_gp(gpsimd):
            gpsimd.memset(ones128[:], 1.0)
            gpsimd.memset(onesrow[:], 1.0)
            gpsimd.memset(zrow[:], 0.0)
            gpsimd.memset(ones8t[:], 1.0)
            gpsimd.memset(epsc[:], EPS)
            gpsimd.memset(ones1t[:], 0.3)
            gpsimd.iota(idnf[:], pattern=[[1, P]], base=0,
                        channel_multiplier=-1,
                        allow_small_or_imprecise_dtypes=True)
            gpsimd.memset(oh32[:], 0.0)
            gpsimd.memset(Hcar[:], 0.0)
            gpsimd.memset(ycar[:], 0.0)
            gpsimd.iota(iotaW1[:], pattern=[[P, NCH]], base=0,
                        channel_multiplier=1,
                        allow_small_or_imprecise_dtypes=True)
            gpsimd.iota(iotaW2[:], pattern=[[P, NF]], base=0,
                        channel_multiplier=1,
                        allow_small_or_imprecise_dtypes=True)
            gpsimd.iota(iotaE1[:], pattern=[[D, E]], base=0,
                        channel_multiplier=0,
                        allow_small_or_imprecise_dtypes=True)
            gpsimd.iota(iotaE2[:], pattern=[[FF, E]], base=0,
                        channel_multiplier=0,
                        allow_small_or_imprecise_dtypes=True)
            GPP.inc(gpsimd.iota(idxR3[:], pattern=[[1, 2 * NB]], base=0,
                                channel_multiplier=0))
            GPP.inc(gpsimd.dma_start(out=wp2_sb[:], in_=Wp2[:]), 16)
        block.gpsimd(pro_gp)
        gpp_mem = 1

        def pro_scalar(scalar):
            ld_act(scalar, SET_LE)
            scalar.wait_ge(LDP.h, ldp_all)
            scalar.activation(out=m8c[:], in_=m8c[:], func=AF.Exp)
            SPR.inc(scalar.activation(out=m8c[:], in_=m8c[:], func=AF.Ln,
                                      bias=1.0))
        block.scalar(pro_scalar)

        def pro_vec(vector):
            vector.wait_ge(SPR.h, SPR.n)
            vector.wait_ge(GPP.h, 1)
            vector.tensor_scalar_mul(m8c[:], m8c[:], -8.0)
            VPR.inc(vector.tensor_single_scalar(out=idnf[:], in_=idnf[:],
                                                scalar=0.0, op=ALU.is_equal))
            vector.wait_ge(VPR.h, VPR.n)
            vector.tensor_copy(out=idnb[:], in_=idnf[:])
            VPR.inc(vector.tensor_scalar_mul(bp1s_sb[:], bp1_sb[:], RS2))
        block.vector(pro_vec)
        vpr_all = VPR.n

        with ExitStack() as actx:
            import dataclasses as _dc

            def asb(name, shape, dtype):
                return actx.enter_context(nc.sbuf_tensor(name, shape, dtype))

            def bcast_c(col, n=NCH, w=TA):
                ap = col[:, 0:w]
                return _dc.replace(ap, ap=[list(ap.ap[0]), [0, n]] +
                                   [list(d) for d in ap.ap[1:]])

            Ax = [asb(f"Axs{i}", [P, NCH * TA], F32) for i in range(2)]
            Axbf = asb("Axbf", [P, NCH * TA], BF16)
            Asq = [asb(f"Asq{i}", [P, NCH * TA], BF16) for i in range(3)]
            Ay = asb("Ay", [P, NCH * TA], F32)
            Ax1 = asb("Ax1", [P, NCH * TA], F32)
            Agx = asb("Agx", [P, NCH * TA], F32)
            Ala = asb("Ala", [P, NCH * TA], F32)
            AH = asb("AH", [P, NCH * TA], F32)
            Ax1bf = asb("Ax1bf", [P, NCH * TA], BF16)
            Lconst = asb("Lconst", [P, TA], F32)
            mean1 = [asb(f"mean1{i}", [P, TA], F32) for i in range(2)]
            vvb = asb("vvb", [P, TA], F32)
            ri1 = [asb(f"ri1{i}", [P, TA], F32) for i in range(2)]
            mn1 = [asb(f"mn1{i}", [P, TA], F32) for i in range(2)]
            ri2 = asb("ri2", [P, TA], F32)
            mn2 = asb("mn2", [P, TA], F32)
            ri3 = asb("ri3", [P, TA], F32)
            mn3 = asb("mn3", [P, TA], F32)
            psA = [actx.enter_context(nc.psum_tensor(f"psA{i}", [P, TA], F32))
                   for i in range(2)]
            psB = [actx.enter_context(nc.psum_tensor(f"psB{i}", [P, TA], F32))
                   for i in range(2)]
            psG = [actx.enter_context(nc.psum_tensor(f"psG{i}", [P, TA], F32))
                   for i in range(3)]
            psRL = actx.enter_context(nc.psum_tensor("psRL", [P, E], F32))
            psR = psRL[:, 0:1]
            psL = psRL[0:1, 0:E]

            igw_sb = asb("igw_sb", [P, NCH * P], BF16)
            agw_sb = asb("agw_sb", [P, NCH * P], BF16)
            wp1_sb = asb("wp1_sb", [P, NCH * P], BF16)

            def pro_gp2(gpsimd):
                gpsimd.memset(Lconst[:], LCL)
                GPP.inc(gpsimd.dma_start(
                    out=igw_sb[:].rearrange("p (h j) -> p h j", h=NCH),
                    in_=igw.rearrange("h i j -> i h j")), 16)
                GPP.inc(gpsimd.dma_start(
                    out=agw_sb[:].rearrange("p (h j) -> p h j", h=NCH),
                    in_=agw.rearrange("h i j -> i h j")), 16)
                GPP.inc(gpsimd.dma_start(
                    out=wp1_sb[:].rearrange("p (c j) -> p c j", c=NCH),
                    in_=Wp1.rearrange("(c p) j -> p c j", p=P)), 16)
            block.gpsimd(pro_gp2)
            gpp_dma = GPP.n
            gpp_lc = GPP.n

            def ch(buf, c, w=TA):
                return buf[:, c * w:(c + 1) * w]

            NG = 2
            GW = NCH // NG * TA

            def gr(buf, g):
                return buf[:, g * GW:(g + 1) * GW]

            ntiles = nb * nta

            def emit_prep(tau):
                b = tau % nb
                t0 = (tau // nb) * TA
                par = tau % 2
                xs = Ax[par]

                def p_load(sync, b=b, t0=t0, tau=tau, xs=xs):
                    if tau >= 2:
                        sync.wait_ge(V_X2.h, mst[("vx2", tau - 2)])
                    LDX.inc(sync.dma_start(
                        out=xs[:].rearrange("p (c t) -> p c t", c=NCH),
                        in_=xT[b, :, t0:t0 + TA].rearrange(
                            "(c p) t -> p c t", p=P)), 16)
                block.sync(p_load)
                mst[("ldx", tau)] = LDX.n

                def p_cast(gpsimd, tau=tau, xs=xs):
                    gpsimd.wait_ge(LDX.h, mst[("ldx", tau)])
                    if tau >= 1:
                        gpsimd.wait_ge(T_ST1.h, mst[("tst1", tau - 1)])
                    GXB.inc(gpsimd.dma_start(out=Axbf[:], in_=xs[:]), 16)
                block.gpsimd(p_cast)
                mst[("gxb", tau)] = GXB.n

                def p_sq(scalar, tau=tau, xs=xs):
                    scalar.wait_ge(LDX.h, mst[("ldx", tau)])
                    if tau >= 1:
                        scalar.wait_ge(T_ST1.h, mst[("tst1", tau - 1)])
                    S_SQ.inc(scalar.activation(out=Asq[0][:], in_=xs[:],
                                               func=AF.Square))
                block.scalar(p_sq)
                mst[("ssq", tau)] = S_SQ.n

                def p_st1(tensor, tau=tau, par=par):
                    if tau == 0:
                        tensor.wait_ge(GPP.h, gpp_mem)
                    tensor.wait_ge(GXB.h, mst[("gxb", tau)])
                    tensor.wait_ge(S_SQ.h, mst[("ssq", tau)])
                    if tau >= 2:
                        tensor.wait_ge(V_ST2.h, mst[("vst2", tau - 2)])
                    for c in range(NCH):
                        tensor.matmul(out=psA[par][:], lhsT=ones128[:],
                                      rhs=ch(Axbf, c), start=(c == 0),
                                      stop=(c == NCH - 1))
                    ins = None
                    for c in range(NCH):
                        ins = tensor.matmul(out=psB[par][:], lhsT=ones128[:],
                                            rhs=ch(Asq[0], c),
                                            start=(c == 0),
                                            stop=(c == NCH - 1))
                    T_ST1.inc(ins)
                block.tensor(p_st1)
                mst[("tst1", tau)] = T_ST1.n

            def emit_vst1(tau):
                par = tau % 2

                def p_vst1(vector, tau=tau, par=par):
                    vector.wait_ge(T_ST1.h, mst[("tst1", tau)])
                    vector.tensor_scalar(out=mean1[par][:], in0=psA[par][:],
                                         scalar1=1.0 / D, scalar2=None,
                                         op0=ALU.mult)
                    vector.scalar_tensor_tensor(out=vvb[:],
                                                in0=mean1[par][:],
                                                scalar=-1.0,
                                                in1=mean1[par][:],
                                                op0=ALU.mult, op1=ALU.mult)
                    V_ST1.inc(vector.scalar_tensor_tensor(
                        out=vvb[:], in0=psB[par][:], scalar=1.0 / D,
                        in1=vvb[:], op0=ALU.mult, op1=ALU.add))
                block.vector(p_vst1)
                mst[("vst1", tau)] = V_ST1.n

            def emit_ri1(tau):
                par = tau % 2

                def p_ri1(scalar, tau=tau, par=par):
                    scalar.wait_ge(V_ST1.h, mst[("vst1", tau)])
                    scalar.activation(out=ri1[par][:], in_=vvb[:],
                                      func=AF.Ln, bias=epsc[:, 0:1])
                    S_RI1.inc(scalar.activation(out=ri1[par][:],
                                                in_=ri1[par][:],
                                                func=AF.Exp, scale=-0.5))
                block.scalar(p_ri1)
                mst[("sri1", tau)] = S_RI1.n

            def emit_tail_vst2(tz):
                parz = tz % 2

                def a_vst2(vector, tz=tz, parz=parz):
                    vector.wait_ge(T_ST2.h, mst[("tst2", tz)])
                    vector.tensor_copy(out=mn1[parz][:], in_=mean1[parz][:])
                    vector.tensor_scalar(out=ri2[:], in0=psA[parz][:],
                                         scalar1=1.0 / D, scalar2=None,
                                         op0=ALU.mult)
                    vector.scalar_tensor_tensor(out=mn2[:], in0=ri2[:],
                                                scalar=-1.0, in1=ri2[:],
                                                op0=ALU.mult, op1=ALU.mult)
                    V_ST2.inc(vector.scalar_tensor_tensor(
                        out=mn2[:], in0=psB[parz][:], scalar=1.0 / D,
                        in1=mn2[:], op0=ALU.mult, op1=ALU.add))
                block.vector(a_vst2)
                mst[("vst2", tz)] = V_ST2.n

            def emit_tail_ri2(tz):
                def a_ri2(scalar, tz=tz):
                    scalar.wait_ge(V_ST2.h, mst[("vst2", tz)])
                    scalar.activation(out=mn2[:], in_=mn2[:], func=AF.Ln,
                                      bias=epsc[:, 0:1])
                    S_RI2.inc(scalar.activation(out=mn2[:], in_=mn2[:],
                                                func=AF.Exp, scale=-0.5))
                block.scalar(a_ri2)
                mst[("sri2", tz)] = S_RI2.n

            def emit_tail_apply(tz):
                bz = tz % nb
                tz0 = (tz // nb) * TA
                xsz = Ax[tz % 2]

                def a_ap2v(vector, tz=tz):
                    vector.wait_ge(S_RI2.h, mst[("sri2", tz)])
                    if dbg:
                        vector.wait_ge(DBGY.h, mst[("dbgy", tz)])
                    vector.scalar_tensor_tensor(out=ri2[:], in0=ri2[:],
                                                scalar=-1.0, in1=mn2[:],
                                                op0=ALU.mult, op1=ALU.mult)
                    for g in range(NG):
                        vector.tensor_tensor(
                            out=gr(Ay, g).rearrange("p (c t) -> p c t",
                                                    t=TA),
                            in0=gr(Ay, g).rearrange("p (c t) -> p c t",
                                                    t=TA),
                            in1=bcast_c(mn2, n=NCH // NG),
                            op=ALU.mult)
                        ins = vector.tensor_tensor(
                            out=gr(Ay, g).rearrange("p (c t) -> p c t",
                                                    t=TA),
                            in0=gr(Ay, g).rearrange("p (c t) -> p c t",
                                                    t=TA),
                            in1=bcast_c(ri2, n=NCH // NG),
                            op=ALU.add)
                        V_AP2.inc(ins)
                        mst[("vap2", tz, g)] = V_AP2.n
                block.vector(a_ap2v)

                def a_x2(vector, tz=tz, xsz=xsz):
                    ins = None
                    for g in range(NG):
                        ins = vector.tensor_add(gr(Ay, g), gr(Ay, g),
                                                gr(xsz, g))
                    V_X2.inc(ins)
                    vector.wait_ge(V_X2.h, V_X2.n)
                    V_X2.inc(vector.tensor_copy(
                        out=x2lsb[:, 0:NCH],
                        in_=Ay[:].rearrange("p (c t) -> p c t",
                                            c=NCH)[:, :, TA - 1]))
                block.vector(a_x2)
                mst[("vx2", tz)] = V_X2.n

                def a_spill(sync, bz=bz, tz0=tz0, tz=tz):
                    sync.wait_ge(V_X2.h, mst[("vx2", tz)])
                    SPL.inc(sync.dma_start(
                        out=x2sp[bz, :, tz0:tz0 + TA].rearrange(
                            "(c p) t -> p c t", p=P),
                        in_=Ay[:].rearrange("p (c t) -> p c t", c=NCH)),
                        16)
                block.sync(a_spill)
                mst[("spl", tz)] = SPL.n

            def emit_tail_sqx2(tz):
                parz = tz % 2

                def a_sqx2(scalar, tz=tz, parz=parz):
                    scalar.wait_ge(V_X2.h, mst[("vx2", tz)])
                    if tz >= 1:
                        scalar.wait_ge(T_ST3.h, mst[("tst3", tz - 1)])
                    S_SQX2.inc(scalar.activation(out=Asq[2][:],
                                                 in_=Ay[:],
                                                 func=AF.Square))
                block.scalar(a_sqx2)
                mst[("sqx2", tz)] = S_SQX2.n

            def emit_tail_st3(tz):
                parz = tz % 2

                def a_st3(tensor, tz=tz, parz=parz):
                    tensor.wait_ge(S_SQX2.h, mst[("sqx2", tz)])
                    tensor.wait_ge(V_ST2.h, mst[("vst2", tz)])
                    if tz + 2 < ntiles:
                        tensor.wait_ge(V_ST1.h, tz + 3)
                    ins = None
                    for c in range(NCH):
                        ins = tensor.matmul(out=psB[parz][:],
                                            lhsT=ones128[:],
                                            rhs=ch(Asq[2], c),
                                            start=(c == 0),
                                            stop=(c == NCH - 1))
                    T_ST3.inc(ins)
                block.tensor(a_st3)
                mst[("tst3", tz)] = T_ST3.n

            def emit_tail_ln3(tz):
                parz = tz % 2
                bz = tz % nb
                tz0 = (tz // nb) * TA

                def a_vst3(vector, tz=tz, parz=parz):
                    vector.wait_ge(T_ST3.h, mst[("tst3", tz)])
                    if tz > 0:
                        vector.wait_ge(SPL3.h, mst[("spl3", tz - 1)])
                    vector.scalar_tensor_tensor(out=ri3[:],
                                                in0=mn1[parz][:],
                                                scalar=-1.0,
                                                in1=mn1[parz][:],
                                                op0=ALU.mult, op1=ALU.mult)
                    V_ST3.inc(vector.scalar_tensor_tensor(
                        out=ri3[:], in0=psB[parz][:], scalar=1.0 / D,
                        in1=ri3[:], op0=ALU.mult, op1=ALU.add))
                block.vector(a_vst3)
                mst[("vst3", tz)] = V_ST3.n

                def a_ri3(scalar, tz=tz):
                    scalar.wait_ge(V_ST3.h, mst[("vst3", tz)])
                    scalar.activation(out=ri3[:], in_=ri3[:], func=AF.Ln,
                                      bias=epsc[:, 0:1])
                    S_RI3.inc(scalar.activation(out=ri3[:], in_=ri3[:],
                                                func=AF.Exp, scale=-0.5))
                block.scalar(a_ri3)
                mst[("sri3", tz)] = S_RI3.n

                def a_mn3(vector, tz=tz, parz=parz):
                    vector.wait_ge(S_RI3.h, mst[("sri3", tz)])
                    ins = vector.scalar_tensor_tensor(
                        out=mn3[:], in0=mn1[parz][:], scalar=-1.0,
                        in1=ri3[:], op0=ALU.mult, op1=ALU.mult)
                    V_MN3.inc(ins)
                block.vector(a_mn3)
                mst[("vmn3", tz)] = V_MN3.n

                def a_r3out(sync, bz=bz, tz0=tz0, tz=tz):
                    sync.wait_ge(V_MN3.h, mst[("vmn3", tz)])
                    SPL3.inc(sync.dma_start(out=r3sp[bz, 0:1, tz0:tz0 + TA],
                                            in_=ri3[0:1, :]), 16)
                    SPL3.inc(sync.dma_start(out=r3sp[bz, 1:2, tz0:tz0 + TA],
                                            in_=mn3[0:1, :]), 16)
                block.sync(a_r3out)
                mst[("spl3", tz)] = SPL3.n

            def emit_router(bz, tz):
                def rt_v1(vector, bz=bz, tz=tz):
                    vector.wait_ge(V_MN3.h, mst[("vmn3", tz)])
                    if bz > 0:
                        vector.wait_ge(GX3L.h, mst[("gx3l", bz - 1)])
                    ins = vector.tensor_tensor(
                        out=x3lsb[:, 0:NCH],
                        in0=x2lsb[:, 0:NCH],
                        in1=_dc.replace(ri3[:, TA - 1:TA],
                                        ap=[list(ri3[:, 0:1].ap[0]),
                                            [0, NCH]]),
                        op=ALU.mult)
                    V_RT.inc(ins)
                    vector.wait_ge(V_RT.h, V_RT.n)
                    ins = vector.tensor_tensor(
                        out=x3lsb[:, 0:NCH], in0=x3lsb[:, 0:NCH],
                        in1=_dc.replace(mn3[:, TA - 1:TA],
                                        ap=[list(mn3[:, 0:1].ap[0]),
                                            [0, NCH]]),
                        op=ALU.add)
                    V_RT.inc(ins)
                block.vector(rt_v1)
                mst[("vrt1", bz)] = V_RT.n

                def rt_g1(gpsimd, bz=bz):
                    gpsimd.wait_ge(V_RT.h, mst[("vrt1", bz)])
                    if bz > 0:
                        gpsimd.wait_ge(T_RT.h, mst[("trt1", bz - 1)])
                    GX3L.inc(gpsimd.dma_start(out=x3lbf[:], in_=x3lsb[:]),
                             16)
                block.gpsimd(rt_g1)
                mst[("gx3l", bz)] = GX3L.n

                def rt_t1(tensor, bz=bz):
                    tensor.wait_ge(GX3L.h, mst[("gx3l", bz)])
                    if bz > 0:
                        tensor.wait_ge(RTD.h, mst[("rtd", bz - 1)])
                    ins = None
                    for c in range(NCH):
                        ins = tensor.matmul(
                            out=psR, lhsT=wp1_sb[:, c * P:(c + 1) * P],
                            rhs=x3lbf[:, c:c + 1], start=(c == 0),
                            stop=(c == NCH - 1))
                    T_RT.inc(ins)
                block.tensor(rt_t1)
                mst[("trt1", bz)] = T_RT.n

                def rt_s1(scalar, bz=bz):
                    ld_act(scalar, SET_SG)
                    scalar.wait_ge(T_RT.h, mst[("trt1", bz)])
                    if bz == 0:
                        scalar.wait_ge(VPR.h, vpr_all)
                    scalar.activation(out=gvec[:], in_=psR,
                                      func=AF.Identity, bias=bp1_sb[:, 0:1])
                    ins = scalar.activation(out=evec[:], in_=psR,
                                            func=AF.Erf, scale=RS2,
                                            bias=bp1s_sb[:, 0:1])
                    S_RT.inc(ins)
                    ld_act(scalar, SET_LE)
                block.scalar(rt_s1)
                mst[("srt1", bz)] = S_RT.n

                def rt_v2(vector, bz=bz):
                    vector.wait_ge(S_RT.h, mst[("srt1", bz)])
                    V_RT.inc(vector.tensor_scalar(
                        out=wvec[:], in0=evec[:], scalar1=0.5, scalar2=0.5,
                        op0=ALU.mult, op1=ALU.add))
                    vector.wait_ge(V_RT.h, V_RT.n)
                    ins = vector.tensor_mul(u1bf[:], wvec[:], gvec[:])
                    V_RT.inc(ins)
                block.vector(rt_v2)
                mst[("vrt2", bz)] = V_RT.n

                def rt_t2(tensor, bz=bz):
                    tensor.wait_ge(V_RT.h, mst[("vrt2", bz)])
                    T_RT.inc(tensor.matmul(out=psL, lhsT=u1bf[:],
                                           rhs=wp2_sb[:], start=True,
                                           stop=True))
                block.tensor(rt_t2)
                mst[("trt2", bz)] = T_RT.n

                def rt_v3(vector, bz=bz):
                    vector.wait_ge(T_RT.h, mst[("trt2", bz)])
                    if bz == 0:
                        vector.wait_ge(LDP.h, ldp_all)
                    V_RT.inc(vector.tensor_add(lgrow[:], psL, bp2_sb[:]))
                    vector.wait_ge(V_RT.h, V_RT.n)
                    V_RT.inc(vector.reduce_max(mxr[:], lgrow[:], axis=AXX))
                    vector.wait_ge(V_RT.h, V_RT.n)
                    V_RT.inc(vector.tensor_scalar(
                        out=ohrow[:], in0=lgrow[:], scalar1=mxr[0:1, 0:1],
                        scalar2=None, op0=ALU.is_ge))
                    vector.wait_ge(V_RT.h, V_RT.n)
                    V_RT.inc(vector.tensor_tensor_scan(
                        out=incl[:], data0=ohrow[:], data1=zrow[:],
                        initial=0.0, op0=ALU.add, op1=ALU.add))
                    vector.wait_ge(V_RT.h, V_RT.n)
                    V_RT.inc(vector.tensor_single_scalar(
                        out=fer[:], in_=incl[:], scalar=1.0,
                        op=ALU.is_equal))
                    vector.wait_ge(V_RT.h, V_RT.n)
                    V_RT.inc(vector.tensor_mul(ohrow[:], ohrow[:], fer[:]))
                    vector.wait_ge(V_RT.h, V_RT.n)
                    vector.tensor_mul(t8a[:], ohrow[:], iotaE1[:])
                    vector.tensor_mul(t8b[:], ohrow[:], iotaE2[:])
                    V_RT.inc(vector.tensor_copy(out=oh32[0:1, 0:E],
                                                in_=ohrow[:]))
                    vector.wait_ge(V_RT.h, V_RT.n)
                    vector.reduce_sum(s1k[:], t8a[:], axis=AXX)
                    vector.reduce_sum(s2k[:], t8b[:], axis=AXX)
                    V_RT.inc(vector.transpose(ohT[:], oh32[:]))
                    vector.wait_ge(V_RT.h, V_RT.n)
                    vector.tensor_copy(out=s1kb[:], in_=s1k[:])
                    vector.tensor_copy(out=s2kb[:], in_=s2k[:])
                    ins = vector.tensor_copy(out=ohcols[:, bz:bz + 1],
                                             in_=ohT[0:E, 0:1])
                    V_RT.inc(ins)
                block.vector(rt_v3)
                mst[("vrt3", bz)] = V_RT.n

                def rt_t3(tensor, bz=bz):
                    tensor.wait_ge(V_RT.h, mst[("vrt3", bz)])
                    T_RT.inc(tensor.matmul(out=psR, lhsT=onesrow[:],
                                           rhs=s1kb[:], start=True,
                                           stop=True))
                block.tensor(rt_t3)
                mst[("trt3", bz)] = T_RT.n

                def rt_v4(vector, bz=bz):
                    vector.wait_ge(T_RT.h, mst[("trt3", bz)])
                    V_RT.inc(vector.tensor_copy(out=selc[:], in_=psR))
                    vector.wait_ge(V_RT.h, V_RT.n)
                    V_RT.inc(vector.tensor_scalar(
                        out=idx1f[:], in0=iotaW1[:], scalar1=selc[:, 0:1],
                        scalar2=None, op0=ALU.add))
                    vector.wait_ge(V_RT.h, V_RT.n)
                    ins = vector.tensor_copy(
                        out=idx1u[:, bz * NCH:(bz + 1) * NCH], in_=idx1f[:])
                    V_RT.inc(ins)
                block.vector(rt_v4)
                mst[("vrt4", bz)] = V_RT.n

                def rt_t4(tensor, bz=bz):
                    tensor.wait_ge(V_RT.h, mst[("vrt4", bz)])
                    T_RT.inc(tensor.matmul(out=psR, lhsT=onesrow[:],
                                           rhs=s2kb[:], start=True,
                                           stop=True))
                    if bz == nb - 1:
                        BARR.inc(tensor.matmul(
                            out=psL[0:1, 0:1], lhsT=onesrow[:, 0:1],
                            rhs=s2kb[:], start=True, stop=True))
                block.tensor(rt_t4)
                mst[("trt4", bz)] = T_RT.n

                def rt_v5(vector, bz=bz):
                    vector.wait_ge(T_RT.h, mst[("trt4", bz)])
                    V_RT.inc(vector.tensor_copy(out=selc[:], in_=psR))
                    vector.wait_ge(V_RT.h, V_RT.n)
                    V_RT.inc(vector.tensor_scalar(
                        out=idx2f[:], in0=iotaW2[:], scalar1=selc[:, 0:1],
                        scalar2=None, op0=ALU.add))
                    vector.wait_ge(V_RT.h, V_RT.n)
                    ins = vector.tensor_copy(
                        out=idx2u[:, bz * NF:(bz + 1) * NF], in_=idx2f[:])
                    RTD.inc(ins)
                block.vector(rt_v5)
                mst[("rtd", bz)] = RTD.n

                if dbg:
                    def rt_dbg(sync, bz=bz):
                        sync.wait_ge(RTD.h, mst[("rtd", bz)])
                        OUT.inc(sync.dma_start(
                            out=dbg_rt[bz:bz + 1, 0:E], in_=lgrow[:]), 16)
                        OUT.inc(sync.dma_start(
                            out=dbg_rt[bz:bz + 1, E:2 * E], in_=ohrow[:]),
                            16)
                    block.sync(rt_dbg)

            emit_prep(0)
            emit_vst1(0)
            emit_ri1(0)

            for tau in range(ntiles):
                b = tau % nb
                ti = tau // nb
                par = tau % 2
                t0 = ti * TA
                xs = Ax[par]

                def a_x1(vector, tau=tau, xs=xs, par=par):
                    vector.wait_ge(S_RI1.h, mst[("sri1", tau)])
                    if tau >= 1:
                        vector.wait_ge(GX1B.h, mst[("gx1b", tau - 1)])
                    vector.scalar_tensor_tensor(out=mn1[par][:],
                                                in0=mean1[par][:],
                                                scalar=-1.0,
                                                in1=ri1[par][:],
                                                op0=ALU.mult, op1=ALU.mult)
                    vector.tensor_tensor(
                        out=Ax1[:].rearrange("p (c t) -> p c t", c=NCH),
                        in0=xs[:].rearrange("p (c t) -> p c t", c=NCH),
                        in1=bcast_c(ri1[par]), op=ALU.mult)
                    ins = None
                    for g in range(NG):
                        ins = vector.tensor_tensor(
                            out=gr(Ax1, g).rearrange("p (c t) -> p c t",
                                                     t=TA),
                            in0=gr(Ax1, g).rearrange("p (c t) -> p c t",
                                                     t=TA),
                            in1=bcast_c(mn1[par], n=NCH // NG),
                            op=ALU.add)
                    V_X1.inc(ins)
                block.vector(a_x1)
                mst[("vx1", tau)] = V_X1.n

                if tau >= 1:
                    emit_tail_vst2(tau - 1)
                    emit_tail_ri2(tau - 1)

                def a_gx1(gpsimd, tau=tau):
                    gpsimd.wait_ge(V_X1.h, mst[("vx1", tau)])
                    if tau > 0:
                        gpsimd.wait_ge(T_ST2.h, mst[("tst2", tau - 1)])
                    GX1B.inc(gpsimd.dma_start(out=Ax1bf[:], in_=Ax1[:]), 16)
                block.gpsimd(a_gx1)
                mst[("gx1b", tau)] = GX1B.n

                ssig_base = S_SIG.n

                def a_gates(tensor, tau=tau, ssig_base=ssig_base):
                    if tau == 0:
                        tensor.wait_ge(GPP.h, gpp_dma)
                    tensor.wait_ge(GX1B.h, mst[("gx1b", tau)])
                    for c in range(NCH):
                        if c >= 1:
                            tensor.wait_ge(S_SIG.h, ssig_base + 2 * c - 2)
                        elif tau > 0:
                            tensor.wait_ge(S_SIG.h, ssig_base)
                        tensor.matmul(out=psG[(2 * c) % 3][:],
                                      lhsT=igw_sb[:, c * P:(c + 1) * P],
                                      rhs=ch(Ax1bf, c), start=True,
                                      stop=True)
                        if c >= 1:
                            tensor.wait_ge(S_SIG.h, ssig_base + 2 * c - 1)
                        ins = tensor.matmul(
                            out=psG[(2 * c + 1) % 3][:],
                            lhsT=agw_sb[:, c * P:(c + 1) * P],
                            rhs=ch(Ax1bf, c), start=True, stop=True)
                        T_G.inc(ins)
                        mst[("tg", tau, c)] = T_G.n
                block.tensor(a_gates)

                if tau >= 1:
                    emit_tail_apply(tau - 1)

                def a_sig(scalar, tau=tau):
                    ld_act(scalar, SET_SG)
                    if tau == 0:
                        scalar.wait_ge(LDP.h, ldp_all)
                    else:
                        scalar.wait_ge(V_YS.h, mst[("vys", tau - 1)])
                        scalar.wait_ge(V_D.h, mst[("vd", tau - 1)])
                    for c in range(NCH):
                        scalar.wait_ge(T_G.h, mst[("tg", tau, c)])
                        S_SIG.inc(scalar.activation(
                            out=ch(Agx, c), in_=psG[(2 * c) % 3][:],
                            func=AF.Sigmoid, bias=igb_sb[:, c:c + 1]))
                        ins = scalar.activation(out=ch(Ala, c),
                                                in_=psG[(2 * c + 1) % 3][:],
                                                func=AF.Sigmoid,
                                                bias=agb_sb[:, c:c + 1])
                        S_SIG.inc(ins)
                        mst[("ssig", tau, c)] = S_SIG.n
                    ld_act(scalar, SET_LE)
                    mst[("ssig_all", tau)] = S_SIG.n
                block.scalar(a_sig)

                if tau >= 1:
                    emit_tail_sqx2(tau - 1)

                def a_la(vector, tau=tau):
                    if tau == 0:
                        vector.wait_ge(VPR.h, vpr_all)
                    ins = None
                    for c in range(NCH):
                        vector.wait_ge(S_SIG.h, mst[("ssig", tau, c)])
                        ins = vector.tensor_scalar(
                            out=ch(Ala, c), in0=ch(Ala, c),
                            scalar1=m8c[:, c:c + 1], scalar2=None,
                            op0=ALU.mult)
                    V_LA.inc(ins)
                block.vector(a_la)
                mst[("vla", tau)] = V_LA.n

                def a_hs(vector, b=b, ti=ti, tau=tau):
                    if tau == 0:
                        vector.wait_ge(GPP.h, gpp_lc)
                    if tau > 0:
                        vector.wait_ge(S_MUL.h, mst[("smul", tau - 1)])
                    hb = b * 2 * NCH
                    ins = None
                    for c in range(NCH):
                        ins = vector.tensor_tensor_scan(
                            out=ch(AH, c), data0=ch(Ala, c),
                            data1=Lconst[:],
                            initial=Hcar[:, hb + 2 * c + 1:hb + 2 * c + 2],
                            op0=ALU.add, op1=ALU.max)
                    V_HS.inc(ins)
                block.vector(a_hs)
                mst[("vhs", tau)] = V_HS.n

                def a_d(vector, tau=tau, b=b):
                    if tau > 0:
                        vector.wait_ge(S_EXP.h, mst[("sexp", tau - 1)])
                        vector.wait_ge(SPL.h, mst[("spl", tau - 1)])
                        vector.wait_ge(S_SQX2.h, mst[("sqx2", tau - 1)])
                        vector.wait_ge(GYB.h, mst[("gyb", tau - 1)])
                        if dbg:
                            vector.wait_ge(DBGY.h, mst[("dbgy", tau - 1)])
                    Hv = AH[:].rearrange("p (c t) -> p c t", c=NCH)
                    dv = Ay[:].rearrange("p (c t) -> p c t", c=NCH)
                    Hodd = Hcar[:, b * 2 * NCH:(b + 1) * 2 * NCH].rearrange(
                        "p (c k) -> p c k", k=2)
                    vector.tensor_sub(dv[:, :, 0:1], Hodd[:, :, 1:2],
                                      Hodd[:, :, 0:1])
                    vector.tensor_sub(dv[:, :, 1:2], Hv[:, :, 0:1],
                                      Hodd[:, :, 1:2])
                    vector.tensor_sub(dv[:, :, 2:TA], Hv[:, :, 1:TA - 1],
                                      Hv[:, :, 0:TA - 2])
                    vector.tensor_copy(out=Hodd[:, :, 0:1],
                                       in_=Hv[:, :, TA - 2:TA - 1])
                    ins = vector.tensor_copy(out=Hodd[:, :, 1:2],
                                             in_=Hv[:, :, TA - 1:TA])
                    V_D.inc(ins)
                block.vector(a_d)
                mst[("vd", tau)] = V_D.n

                sexp_base = S_EXP.n

                def a_exp_a2(scalar, tau=tau):
                    scalar.wait_ge(V_HS.h, mst[("vhs", tau)])
                    for g in range(NG):
                        S_EXP.inc(scalar.activation(out=gr(Ala, g),
                                                    in_=gr(Ala, g),
                                                    func=AF.Exp, scale=2.0))
                block.scalar(a_exp_a2)

                def a_m2(vector, tau=tau, sexp_base=sexp_base):
                    for g in range(NG):
                        vector.wait_ge(S_EXP.h, sexp_base + g + 1)
                        vector.tensor_scalar(out=gr(AH, g), in0=gr(Ala, g),
                                             scalar1=-1.0, scalar2=1.0,
                                             op0=ALU.mult, op1=ALU.add)
                        ins = vector.tensor_scalar_max(gr(AH, g),
                                                       gr(AH, g), 1e-7)
                        V_M2.inc(ins)
                        mst[("vm2", tau, g)] = V_M2.n
                block.vector(a_m2)
                mst[("vm2", tau)] = V_M2.n

                def a_mul(scalar, tau=tau):
                    ins = None
                    for g in range(NG):
                        scalar.wait_ge(V_M2.h, mst[("vm2", tau, g)])
                        scalar.activation(out=gr(AH, g), in_=gr(AH, g),
                                          func=AF.Ln)
                        ins = scalar.activation(out=gr(AH, g), in_=gr(AH, g),
                                                func=AF.Exp, scale=0.5)
                        S_MUL.inc(ins)
                        mst[("smul", tau, g)] = S_MUL.n
                block.scalar(a_mul)
                mst[("smul", tau)] = S_MUL.n

                def a_exp_A(scalar, tau=tau):
                    scalar.wait_ge(V_D.h, mst[("vd", tau)])
                    for g in range(NG):
                        S_EXP.inc(scalar.activation(out=gr(Ay, g),
                                                    in_=gr(Ay, g),
                                                    func=AF.Exp))
                block.scalar(a_exp_A)
                mst[("sexp", tau)] = S_EXP.n

                def a_ys(vector, tau=tau, b=b):
                    for g in range(NG):
                        vector.wait_ge(S_MUL.h, mst[("smul", tau, g)])
                        vector.tensor_mul(gr(Agx, g), gr(Agx, g),
                                          gr(AH, g))
                        vector.tensor_mul(gr(Ax1, g), gr(Ax1, g),
                                          gr(Agx, g))
                    vector.wait_ge(S_EXP.h, mst[("sexp", tau)])
                    yb = b * NCH
                    ins = None
                    for c in range(NCH):
                        ins = vector.tensor_tensor_scan(
                            out=ch(Ay, c), data0=ch(Ay, c),
                            data1=ch(Ax1, c),
                            initial=ycar[:, yb + c:yb + c + 1], op0=ALU.mult,
                            op1=ALU.add)
                    V_YS.inc(ins)
                    mst[("vys", tau)] = V_YS.n
                    vector.wait_ge(V_YS.h, mst[("vys", tau)])
                    ins = vector.tensor_copy(
                        out=ycar[:, yb:yb + NCH],
                        in_=Ay[:].rearrange("p (c t) -> p c t",
                                            c=NCH)[:, :, TA - 1])
                    V_CAR.inc(ins)
                block.vector(a_ys)
                mst[("vcar", tau)] = V_CAR.n

                if tau + 1 < ntiles:
                    emit_prep(tau + 1)
                    emit_vst1(tau + 1)
                    emit_ri1(tau + 1)
                if tau >= 1:
                    emit_tail_st3(tau - 1)
                    emit_tail_ln3(tau - 1)
                    if (tau - 1) // nb == nta - 1:
                        emit_router((tau - 1) % nb, tau - 1)

                if dbg:
                    def a_dbgy(sync, b=b, t0=t0, tau=tau):
                        sync.wait_ge(V_YS.h, mst[("vys", tau)])
                        DBGY.inc(sync.dma_start(
                            out=dbg_y[b, :, t0:t0 + TA].rearrange(
                                "(c p) t -> p c t", p=P),
                            in_=Ay[:].rearrange("p (c t) -> p c t",
                                                c=NCH)), 16)
                    block.sync(a_dbgy)
                    mst[("dbgy", tau)] = DBGY.n

                def a_yb(gpsimd, tau=tau):
                    gpsimd.wait_ge(V_YS.h, mst[("vys", tau)])
                    if tau > 0:
                        gpsimd.wait_ge(T_ST2.h, mst[("tst2", tau - 1)])
                    GYB.inc(gpsimd.dma_start(out=Ax1bf[:], in_=Ay[:]), 16)
                block.gpsimd(a_yb)
                mst[("gyb", tau)] = GYB.n

                def a_sqy(scalar, tau=tau):
                    scalar.wait_ge(V_YS.h, mst[("vys", tau)])
                    if tau >= 1:
                        scalar.wait_ge(T_ST2.h, mst[("tst2", tau - 1)])
                    if dbg:
                        scalar.wait_ge(DBGY.h, mst[("dbgy", tau)])
                    S_SQY.inc(scalar.activation(out=Asq[1][:], in_=Ay[:],
                                                func=AF.Square))
                block.scalar(a_sqy)
                mst[("ssqy", tau)] = S_SQY.n

                def a_st2(tensor, tau=tau, par=par):
                    tensor.wait_ge(GYB.h, mst[("gyb", tau)])
                    tensor.wait_ge(S_SQY.h, mst[("ssqy", tau)])
                    tensor.wait_ge(V_ST1.h, mst[("vst1", tau)])
                    if tau >= 2:
                        tensor.wait_ge(V_ST3.h, mst[("vst3", tau - 2)])
                    for c in range(NCH):
                        tensor.matmul(out=psA[par][:], lhsT=ones128[:],
                                      rhs=ch(Ax1bf, c), start=(c == 0),
                                      stop=(c == NCH - 1))
                    ins = None
                    for c in range(NCH):
                        ins = tensor.matmul(out=psB[par][:],
                                            lhsT=ones128[:],
                                            rhs=ch(Asq[1], c),
                                            start=(c == 0),
                                            stop=(c == NCH - 1))
                    T_ST2.inc(ins)
                block.tensor(a_st2)
                mst[("tst2", tau)] = T_ST2.n

            def a_gw0(gpsimd):
                gpsimd.wait_ge(RTD.h, mst[("rtd", 0)])
                for kc in range(NCH // 2):
                    GW1.inc(gpsimd.indirect_dma_start(
                        out=w1h0[:, kc * FF:(kc + 1) * FF],
                        out_offset=None, in_=W1f,
                        in_offset=IOX(idx1u[:, kc:kc + 1], 0)), 16)
            block.gpsimd(a_gw0)
            mbseed_gw1h0_b0 = GW1.n

            emit_tail_vst2(ntiles - 1)
            emit_tail_ri2(ntiles - 1)
            emit_tail_apply(ntiles - 1)
            emit_tail_sqx2(ntiles - 1)
            emit_tail_st3(ntiles - 1)
            emit_tail_ln3(ntiles - 1)
            emit_router(nb - 1, ntiles - 1)

        def bar_v(vector):
            BARR.inc(vector.tensor_copy(out=bar1[:], in_=selc[:]))
        block.vector(bar_v)

        def bar_s(scalar):
            BARR.inc(scalar.copy(out=bar2[:], in_=selc[:]))
        block.scalar(bar_s)

        def bar_g(gpsimd):
            BARR.inc(gpsimd.memset(wvec[:], 0.0))
        block.gpsimd(bar_g)

        def bar_y(sync):
            BARR.inc(sync.dma_start(out=evec[:], in_=bp1_sb[:]), 16)
        block.sync(bar_y)
        barr_val = BARR.n

        if not do_b:
            def fin_a(sync):
                sync.wait_ge(SPL.h, SPL.n)
                sync.wait_ge(SPL3.h, SPL3.n)
                sync.wait_ge(BARR.h, barr_val)
                if OUT.n:
                    sync.wait_ge(OUT.h, OUT.n)
                if dbg:
                    sync.wait_ge(DBGY.h, DBGY.n)
            block.sync(fin_a)
            return nc

        with ExitStack() as bctx:
            import dataclasses as _dc2

            def bsb(name, shape, dtype):
                return bctx.enter_context(nc.sbuf_tensor(name, shape, dtype))

            w1h1 = bsb("w1h1", [P, (NCH // 2) * FF], BF16)
            w1h2 = bsb("w1h2", [P, (NCH // 2) * FF], BF16)
            b1all = bsb("b1all", [E, FF], BF16)
            b2all = bsb("b2all", [E, D], BF16)
            bo_r = bsb("bo_r", [1, D], BF16)
            w1slots = [w1h0, w1h1, w1h2]
            w2sb = bsb("w2sb", [P, NF * D], BF16)
            wo_sb = bsb("wo_sb", [P, NCH * D], BF16)
            x3B = bsb("x3B", [P, NCH * TB], F32)
            x3bf = [bsb(f"x3bf{i}", [P, NCH * TB], BF16) for i in range(2)]
            hB = bsb("hB", [P, NF * TB], BF16)
            rB = bsb("rB", [P, NCH * TB], BF16)
            fB = bsb("fB", [P, NCH * TB], F32)
            fbf = bsb("fbf", [P, NCH * TB], BF16)
            sqB = rB
            ohrepB = bsb("ohrepB", [E, TB], BF16)
            R3t = bsb("R3t", [P, TB], F32)
            MN3t = bsb("MN3t", [P, TB], F32)
            mean4 = bsb("mean4", [P, TB], F32)
            vv4 = bsb("vv4", [P, TB], F32)
            ri4 = vv4
            ph = [bctx.enter_context(nc.psum_tensor(f"ph{i}", [P, TB], F32))
                  for i in range(2)]
            pr = [bctx.enter_context(nc.psum_tensor(f"pr{i}", [P, TB], F32))
                  for i in range(2)]
            po = [bctx.enter_context(nc.psum_tensor(f"po{i}", [P, TB], F32))
                  for i in range(2)]
            ps4 = bctx.enter_context(nc.psum_tensor("ps4", [P, TB], F32))
            psq4 = bctx.enter_context(nc.psum_tensor("psq4", [P, TB], F32))

            mb = {("gw1h0", 0): mbseed_gw1h0_b0}
            ntb = T // TB
            tbinfo = {tb: (tb // ntb, (tb % ntb) * TB)
                      for tb in range(nb * ntb)}

            def b_pro(gpsimd):
                gpsimd.wait_ge(BARR.h, barr_val)
                GWLD.inc(gpsimd.dma_start(
                    out=wo_sb[:].rearrange("p (c d) -> p c d", c=NCH),
                    in_=Wo.rearrange("(c p) d -> p c d", p=P)), 16)
                GWLD.inc(gpsimd.dma_start(out=b1all[:], in_=b1d[:]), 16)
                GWLD.inc(gpsimd.dma_start(out=b2all[:], in_=b2d[:]), 16)
                GWLD.inc(gpsimd.dma_start(
                    out=bo_r[:], in_=bod.rearrange("c p -> (c p)")[None, :]),
                    16)
            block.gpsimd(b_pro)
            gwld_all = GWLD.n

            def chB(buf, c, w=TB):
                return buf[:, c * w:(c + 1) * w]

            def w1slot(b, j):
                return (2 * b + j) % 3

            def emit_bprep(tb):
                b, t0 = tbinfo[tb]
                tc = t0 // TB

                def b_ld(sync, b=b, t0=t0, tb=tb):
                    if tb == 0:
                        sync.wait_ge(BARR.h, barr_val)
                    if tb > 0:
                        sync.wait_ge(GX3B.h, mb[("gx3b", tb - 1)])
                    sync.wait_ge(SPL.h, mst[("spl", (nta - 1) * nb + b)])
                    LB.inc(sync.dma_start(
                        out=x3B[:].rearrange("p (c t) -> p c t", c=NCH),
                        in_=x2sp[b, :, t0:t0 + TB].rearrange(
                            "(c p) t -> p c t", p=P)), 16)
                block.sync(b_ld)
                mb[("lb", tb)] = LB.n

                def b_r3(gpsimd, b=b, tc=tc, tb=tb):
                    if tb > 0:
                        gpsimd.wait_ge(V_X3M.h, mb[("vx3m", tb - 1)])
                    gpsimd.wait_ge(SPL3.h, mst[("spl3", (nta - 1) * nb + b)])
                    GR3.inc(gpsimd.indirect_dma_start(
                        out=R3t[:], out_offset=None, in_=r3f,
                        in_offset=IOX(idxR3[:, 2 * b:2 * b + 1], 0),
                        element_offset=tc * TB), 16)
                    GR3.inc(gpsimd.indirect_dma_start(
                        out=MN3t[:], out_offset=None, in_=r3f,
                        in_offset=IOX(idxR3[:, 2 * b + 1:2 * b + 2], 0),
                        element_offset=tc * TB), 16)
                block.gpsimd(b_r3)
                mb[("gr3", tb)] = GR3.n

                def b_x3(vector, tb=tb):
                    vector.wait_ge(LB.h, mb[("lb", tb)])
                    vector.wait_ge(GR3.h, mb[("gr3", tb)])
                    xv = x3B[:].rearrange("p (c t) -> p c t", c=NCH)
                    rb = R3t[:, 0:TB]
                    rb = _dc2.replace(rb, ap=[list(rb.ap[0]), [0, NCH]] +
                                      [list(d) for d in rb.ap[1:]])
                    mnb = MN3t[:, 0:TB]
                    mnb = _dc2.replace(mnb, ap=[list(mnb.ap[0]), [0, NCH]] +
                                       [list(d) for d in mnb.ap[1:]])
                    vector.tensor_tensor(out=xv, in0=xv, in1=rb, op=ALU.mult)
                    ins = vector.tensor_tensor(out=xv, in0=xv, in1=mnb,
                                               op=ALU.add)
                    V_X3M.inc(ins)
                block.vector(b_x3)
                mb[("vx3m", tb)] = V_X3M.n

                def b_x3g(gpsimd, tb=tb):
                    gpsimd.wait_ge(V_X3M.h, mb[("vx3m", tb)])
                    if tb > 1:
                        gpsimd.wait_ge(T_O.h, mb[("to", tb - 2, NCH - 1)])
                    GX3B.inc(gpsimd.dma_start(out=x3bf[tb % 2][:],
                                              in_=x3B[:]), 16)
                block.gpsimd(b_x3g)
                mb[("gx3", tb)] = GX3B.n
                mb[("gx3b", tb)] = GX3B.n

            def emit_tail(tz):
                bz, tz0 = tbinfo[tz]

                def t_ffg(gpsimd, tz=tz):
                    gpsimd.wait_ge(V_FF.h, mb[("vff_all", tz)])
                    if tz > 0:
                        gpsimd.wait_ge(T_S4.h, mb[("ts4", tz - 1)])
                    GFB.inc(gpsimd.dma_start(out=fbf[:], in_=fB[:]), 16)
                block.gpsimd(t_ffg)
                mb[("gfb", tz)] = GFB.n

                def t_sq4(scalar, tz=tz):
                    scalar.wait_ge(V_FF.h, mb[("vff_all", tz)])
                    if tz > 0:
                        scalar.wait_ge(T_S4.h, mb[("ts4", tz - 1)])
                    S_SQ4.inc(scalar.activation(out=sqB[:], in_=fB[:],
                                                func=AF.Square))
                block.scalar(t_sq4)
                mb[("ssq4", tz)] = S_SQ4.n

                def t_st4(tensor, tz=tz):
                    tensor.wait_ge(GFB.h, mb[("gfb", tz)])
                    tensor.wait_ge(S_SQ4.h, mb[("ssq4", tz)])
                    if tz > 0:
                        tensor.wait_ge(V_ST4.h, mb[("vst4", tz - 1)])
                    for c in range(NCH):
                        tensor.matmul(out=ps4[:], lhsT=ones128[:],
                                      rhs=chB(fbf, c), start=(c == 0),
                                      stop=(c == NCH - 1))
                    ins = None
                    for c in range(NCH):
                        ins = tensor.matmul(out=psq4[:], lhsT=ones128[:],
                                            rhs=chB(sqB, c), start=(c == 0),
                                            stop=(c == NCH - 1))
                    T_S4.inc(ins)
                block.tensor(t_st4)
                mb[("ts4", tz)] = T_S4.n

                def t_vst4(vector, tz=tz):
                    vector.wait_ge(T_S4.h, mb[("ts4", tz)])
                    vector.tensor_scalar(out=mean4[:], in0=ps4[:],
                                         scalar1=1.0 / D, scalar2=None,
                                         op0=ALU.mult)
                    vector.scalar_tensor_tensor(out=vv4[:], in0=mean4[:],
                                                scalar=-1.0, in1=mean4[:],
                                                op0=ALU.mult, op1=ALU.mult)
                    V_ST4.inc(vector.scalar_tensor_tensor(
                        out=vv4[:], in0=psq4[:], scalar=1.0 / D, in1=vv4[:],
                        op0=ALU.mult, op1=ALU.add))
                block.vector(t_vst4)
                mb[("vst4", tz)] = V_ST4.n

                def t_ri4(scalar, tz=tz):
                    scalar.wait_ge(V_ST4.h, mb[("vst4", tz)])
                    scalar.activation(out=ri4[:], in_=vv4[:], func=AF.Ln,
                                      bias=epsc[:, 0:1])
                    S_RI4.inc(scalar.activation(out=ri4[:], in_=ri4[:],
                                                func=AF.Exp, scale=-0.5))
                block.scalar(t_ri4)
                mb[("sri4", tz)] = S_RI4.n

                def t_fin1(vector, tz=tz):
                    vector.wait_ge(S_RI4.h, mb[("sri4", tz)])
                    vector.scalar_tensor_tensor(out=mean4[:], in0=mean4[:],
                                                scalar=-1.0, in1=ri4[:],
                                                op0=ALU.mult, op1=ALU.mult)
                    rb = ri4[:, 0:TB]
                    rb = _dc2.replace(rb, ap=[list(rb.ap[0]), [0, NCH]] +
                                      [list(d) for d in rb.ap[1:]])
                    ins = vector.tensor_tensor(
                        out=fB[:].rearrange("p (c t) -> p c t", c=NCH),
                        in0=fB[:].rearrange("p (c t) -> p c t", c=NCH),
                        in1=rb, op=ALU.mult)
                    V_FM.inc(ins)
                    mb[("vfm", tz)] = V_FM.n
                block.vector(t_fin1)

                def t_fin2(gpsimd, tz=tz, bz=bz, tz0=tz0):
                    gpsimd.wait_ge(V_FM.h, mb[("vfm", tz)])
                    ins = None
                    for dc in range(NCH):
                        ins = gpsimd.tensor_add(chB(fB, dc), chB(fB, dc),
                                                mean4[:])
                    G_FM.inc(ins)
                    ACC.inc(gpsimd.dma_start(
                        out=fB[:].rearrange("p (c t) -> p c t", c=NCH),
                        in_=x2sp[bz, :, tz0:tz0 + TB].rearrange(
                            "(c p) t -> p c t", p=P),
                        accum_op=ALU.add), 16)
                block.gpsimd(t_fin2)
                mb[("acc", tz)] = ACC.n

                def t_out(sync, tz=tz, bz=bz, tz0=tz0):
                    sync.wait_ge(ACC.h, mb[("acc", tz)])
                    OUT.inc(sync.dma_start(
                        out=oT[bz, :, tz0:tz0 + TB].rearrange(
                            "(c p) t -> p c t", p=P),
                        in_=fB[:].rearrange("p (c t) -> p c t", c=NCH)), 16)
                block.sync(t_out)
                mb[("out", tz)] = OUT.n

            emit_bprep(0)

            for tb in range(nb * ntb):
                b, t0 = tbinfo[tb]
                tc = t0 // TB

                if tc == 0:
                    def b_gath(gpsimd, b=b):
                        gpsimd.wait_ge(RTD.h, mst[("rtd", b)])
                        gpsimd.wait_ge(BARR.h, barr_val)
                        if b > 0:
                            gpsimd.wait_ge(T_H.h, mb[("th", b * ntb - 1)])
                        s1 = w1slot(b, 1)
                        for kc in range(NCH // 2):
                            kg = NCH // 2 + kc
                            GW1.inc(gpsimd.indirect_dma_start(
                                out=w1slots[s1][:, kc * FF:(kc + 1) * FF],
                                out_offset=None, in_=W1f,
                                in_offset=IOX(
                                    idx1u[:, b * NCH + kg:b * NCH + kg + 1],
                                    0)), 16)
                        mb[("gw1h1", b)] = GW1.n
                        if b + 1 < nb:
                            gpsimd.wait_ge(RTD.h, mst[("rtd", b + 1)])
                            s0 = w1slot(b + 1, 0)
                            for kc in range(NCH // 2):
                                GW1.inc(gpsimd.indirect_dma_start(
                                    out=w1slots[s0][:,
                                                    kc * FF:(kc + 1) * FF],
                                    out_offset=None, in_=W1f,
                                    in_offset=IOX(
                                        idx1u[:, (b + 1) * NCH + kc:
                                              (b + 1) * NCH + kc + 1], 0)),
                                    16)
                        mb[("gw1h0", b + 1)] = GW1.n
                        if b > 0:
                            gpsimd.wait_ge(T_R.h, mb[("tr", b * ntb - 1)])
                        for fc in range(NF):
                            GW2.inc(gpsimd.indirect_dma_start(
                                out=w2sb[:, fc * D:(fc + 1) * D],
                                out_offset=None, in_=W2f,
                                in_offset=IOX(
                                    idx2u[:, b * NF + fc:b * NF + fc + 1],
                                    0)), 16)
                    block.gpsimd(b_gath)
                    mb[("gw2", b)] = GW2.n

                    def b_ohr(vector, b=b):
                        vector.wait_ge(RTD.h, mst[("rtd", b)])
                        if b > 0:
                            vector.wait_ge(T_R.h, mb[("tr", b * ntb - 1)])
                        else:
                            vector.wait_ge(BARR.h, barr_val)
                        ins = vector.tensor_scalar(
                            out=ohrepB[:], in0=ones8t[:],
                            scalar1=ohcols[:, b:b + 1], scalar2=None,
                            op0=ALU.mult)
                        V_OHR.inc(ins)
                    block.vector(b_ohr)
                    mb[("vohr", b)] = V_OHR.n

                sh_base = S_H.n

                def b_w1(tensor, b=b, tb=tb, sh_base=sh_base):
                    tensor.wait_ge(GX3B.h, mb[("gx3b", tb)])
                    tensor.wait_ge(GW1.h, mb[("gw1h0", b)])
                    tensor.wait_ge(GW1.h, mb[("gw1h1", b)])
                    tensor.wait_ge(V_OHR.h, mb[("vohr", b)])
                    if tb == 0:
                        tensor.wait_ge(GWLD.h, gwld_all)
                    for fc in range(NF):
                        if fc >= 2:
                            tensor.wait_ge(S_H.h, sh_base + fc - 1)
                        elif tb > 0:
                            tensor.wait_ge(S_H.h, sh_base)
                        sl = fc % 2
                        base = fc * P
                        for kc in range(NCH):
                            slot = w1slots[w1slot(b, kc // (NCH // 2))]
                            kcl = kc % (NCH // 2)
                            tensor.matmul(
                                out=ph[sl][:],
                                lhsT=slot[:, kcl * FF + base:
                                          kcl * FF + base + P],
                                rhs=chB(x3bf[tb % 2], kc), start=(kc == 0),
                                stop=False)
                        ins = tensor.matmul(
                            out=ph[sl][:], lhsT=b1all[:, base:base + P],
                            rhs=ohrepB[:], start=False, stop=True)
                        T_H.inc(ins)
                        mb[("th", tb, fc)] = T_H.n
                    mb[("th", tb)] = T_H.n
                block.tensor(b_w1)

                def b_gelu(scalar, tb=tb):
                    ld_act(scalar, SET_GE)
                    for fc in range(NF):
                        scalar.wait_ge(T_H.h, mb[("th", tb, fc)])
                        ins = scalar.activation(out=chB(hB, fc),
                                                in_=ph[fc % 2][:],
                                                func=AF.Gelu)
                        S_H.inc(ins)
                        mb[("sh", tb, fc)] = S_H.n
                    ld_act(scalar, SET_LE)
                    mb[("sh_all", tb)] = S_H.n
                block.scalar(b_gelu)

                if tb + 1 < nb * ntb:
                    emit_bprep(tb + 1)
                if tb > 0:
                    emit_tail(tb - 1)

                sr_base = S_R.n

                def b_w2(tensor, b=b, tb=tb, sr_base=sr_base):
                    tensor.wait_ge(S_H.h, mb[("sh_all", tb)])
                    tensor.wait_ge(GW2.h, mb[("gw2", b)])
                    ins = None
                    for hc in range(NCH):
                        if hc >= 2:
                            tensor.wait_ge(S_R.h, sr_base + hc - 1)
                        elif tb > 0:
                            tensor.wait_ge(S_R.h, sr_base)
                        sl = hc % 2
                        base = hc * P
                        for fc in range(NF):
                            tensor.matmul(
                                out=pr[sl][:],
                                lhsT=w2sb[:, fc * D + base:fc * D + base + P],
                                rhs=chB(hB, fc), start=(fc == 0), stop=False)
                        ins = tensor.matmul(
                            out=pr[sl][:], lhsT=b2all[:, base:base + P],
                            rhs=ohrepB[:], start=False, stop=True)
                        T_R.inc(ins)
                        mb[("tr", tb, hc)] = T_R.n
                    mb[("tr", tb)] = T_R.n
                block.tensor(b_w2)

                def b_reas(scalar, tb=tb):
                    for hc in range(NCH):
                        scalar.wait_ge(T_R.h, mb[("tr", tb, hc)])
                        ins = scalar.activation(out=chB(rB, hc),
                                                in_=pr[hc % 2][:],
                                                func=AF.Identity, scale=0.3)
                        S_R.inc(ins)
                        mb[("sr", tb, hc)] = S_R.n
                    mb[("sr_all", tb)] = S_R.n
                block.scalar(b_reas)

                vff_base = V_FF.n

                def b_wo(tensor, tb=tb, vff_base=vff_base):
                    tensor.wait_ge(S_R.h, mb[("sr_all", tb)])
                    for dc in range(NCH):
                        if dc >= 2:
                            tensor.wait_ge(V_FF.h, vff_base + dc - 1)
                        elif tb > 0:
                            tensor.wait_ge(V_FF.h, vff_base)
                        sl = dc % 2
                        for hc in range(NCH):
                            tensor.matmul(
                                out=po[sl][:],
                                lhsT=wo_sb[:, hc * D + dc * P:
                                           hc * D + dc * P + P],
                                rhs=chB(rB, hc), start=(hc == 0), stop=False)
                        tensor.matmul(
                            out=po[sl][:], lhsT=bo_r[:, dc * P:(dc + 1) * P],
                            rhs=ones1t[:], start=False, stop=False)
                        ins = tensor.matmul(
                            out=po[sl][:], lhsT=idnb[:],
                            rhs=chB(x3bf[tb % 2], dc), start=False,
                            stop=True)
                        T_O.inc(ins)
                        mb[("to", tb, dc)] = T_O.n
                block.tensor(b_wo)

                def b_ff(vector, tb=tb):
                    for dc in range(NCH):
                        vector.wait_ge(T_O.h, mb[("to", tb, dc)])
                        if tb > 0 and dc == 0:
                            vector.wait_ge(OUT.h, mb[("out", tb - 1)])
                        ins = vector.tensor_copy(out=chB(fB, dc),
                                                 in_=po[dc % 2][:])
                        V_FF.inc(ins)
                        mb[("vff", tb, dc)] = V_FF.n
                    mb[("vff_all", tb)] = V_FF.n
                block.vector(b_ff)

            emit_tail(nb * ntb - 1)

            def fin(sync):
                sync.wait_ge(OUT.h, OUT.n)
            block.sync(fin)

    return nc


def kernel(**inputs):
    x = np.asarray(inputs["L0_attn_x"], dtype=np.float32)
    W1 = np.ascontiguousarray(np.asarray(inputs["L0_ffn_W1"], dtype=np.float32))
    W2 = np.ascontiguousarray(np.asarray(inputs["L0_ffn_W2"], dtype=np.float32))
    Wo = np.ascontiguousarray(np.asarray(inputs["L0_ffn_Wo"], dtype=np.float32))
    Wp1 = np.ascontiguousarray(np.asarray(inputs["L0_ffn_Wp1"], dtype=np.float32))
    Wp2 = np.ascontiguousarray(np.asarray(inputs["L0_ffn_Wp2"], dtype=np.float32))
    b1 = np.ascontiguousarray(np.asarray(inputs["L0_ffn_b1"], dtype=np.float32))
    b2 = np.ascontiguousarray(np.asarray(inputs["L0_ffn_b2"], dtype=np.float32))
    bp1 = np.ascontiguousarray(
        np.asarray(inputs["L0_ffn_bp1"], dtype=np.float32).reshape(P, 1))
    bp2 = np.ascontiguousarray(
        np.asarray(inputs["L0_ffn_bp2"], dtype=np.float32).reshape(1, E))
    bo = np.ascontiguousarray(
        np.asarray(inputs["L0_ffn_bo"], dtype=np.float32).reshape(NCH, P))
    igw = np.ascontiguousarray(np.asarray(inputs["L0_attn_ig_w"], dtype=np.float32))
    agw = np.ascontiguousarray(np.asarray(inputs["L0_attn_ag_w"], dtype=np.float32))
    igb = np.ascontiguousarray(np.asarray(inputs["L0_attn_ig_b"], dtype=np.float32))
    agb = np.ascontiguousarray(np.asarray(inputs["L0_attn_ag_b"], dtype=np.float32))
    apar = np.ascontiguousarray(
        np.asarray(inputs["L0_attn_a_param"], dtype=np.float32).reshape(NCH, P))

    nc = build_nc()
    in_maps = []
    for i in range(8):
        xTc = np.ascontiguousarray(x[4 * i:4 * i + 4].transpose(0, 2, 1))
        in_maps.append(dict(xT=xTc, W1=W1, W2=W2, Wo=Wo, Wp1=Wp1, Wp2=Wp2,
                            b1d=b1, b2d=b2, igw=igw, agw=agw, igb=igb,
                            agb=agb, apar=apar, bp1d=bp1, bp2d=bp2, bod=bo))
    res = run_bass_kernel_spmd(nc, in_maps, core_ids=list(range(8)))
    out = np.concatenate(
        [np.asarray(r["oT"]).transpose(0, 2, 1) for r in res.results], axis=0)
    return np.ascontiguousarray(out.astype(np.float32))


# version 16
# speedup vs baseline: 1.2731x; 1.0313x over previous
import numpy as np
from contextlib import ExitStack

import concourse.bass as bass
import concourse.mybir as mybir
from concourse.bass_utils import run_bass_kernel_spmd

F32 = mybir.dt.float32
BF16 = mybir.dt.bfloat16
U32 = mybir.dt.uint32
AF = mybir.ActivationFunctionType
ALU = mybir.AluOpType
AXX = mybir.AxisListType.X
IOX = bass.IndirectOffsetOnAxis

P = 128
NCH = 8
D = 1024
T = 2048
NB = 4
TA = 512
TB = 512
E = 8
FF = 2048
NF = 16
EPS = 1e-5
LCL = float(np.log(np.float32(1e-7)))
RS2 = float(1.0 / np.sqrt(2.0))


class Sem:
    def __init__(self, ctx, nc, name):
        self.h = ctx.enter_context(nc.semaphore(name))
        self.n = 0

    def inc(self, instr, k=1):
        instr.then_inc(self.h, k)
        self.n += k
        return self.n


def build_nc(nb=NB, nta=T // TA, do_b=True, dbg=False):
    ntb = T // TB
    nc = bass.Bass("TRN2", target_bir_lowering=False, debug=False)

    from concourse.hw_specs import get_activation_tables
    _tabs = list(get_activation_tables(nc.m.arch).values())
    SET_LE = next(i for i, f in enumerate(_tabs)
                  if AF.Ln in f and AF.Exp in f)
    SET_SG = next(i for i, f in enumerate(_tabs)
                  if AF.Sigmoid in f and AF.Erf in f)
    SET_GE = next(i for i, f in enumerate(_tabs) if AF.Gelu in f)

    def ld_act(eng, sid):
        eng.add_instruction(mybir.InstLoadActFuncSet(
            name=nc.get_next_instruction_name(), act_func_set_id=sid,
            ins=[], outs=[]))

    xT = nc.dram_tensor("xT", [NB, D, T], F32, kind="ExternalInput").ap()
    W1 = nc.dram_tensor("W1", [E, D, FF], F32, kind="ExternalInput").ap()
    W2 = nc.dram_tensor("W2", [E, FF, D], F32, kind="ExternalInput").ap()
    Wo = nc.dram_tensor("Wo", [D, D], F32, kind="ExternalInput").ap()
    Wp1 = nc.dram_tensor("Wp1", [D, P], F32, kind="ExternalInput").ap()
    Wp2 = nc.dram_tensor("Wp2", [P, E], F32, kind="ExternalInput").ap()
    b1d = nc.dram_tensor("b1d", [E, FF], F32, kind="ExternalInput").ap()
    b2d = nc.dram_tensor("b2d", [E, D], F32, kind="ExternalInput").ap()
    igw = nc.dram_tensor("igw", [E, P, P], F32, kind="ExternalInput").ap()
    agw = nc.dram_tensor("agw", [E, P, P], F32, kind="ExternalInput").ap()
    igb = nc.dram_tensor("igb", [NCH, P], F32, kind="ExternalInput").ap()
    agb = nc.dram_tensor("agb", [NCH, P], F32, kind="ExternalInput").ap()
    apar = nc.dram_tensor("apar", [NCH, P], F32, kind="ExternalInput").ap()
    bp1d = nc.dram_tensor("bp1d", [P, 1], F32, kind="ExternalInput").ap()
    bp2d = nc.dram_tensor("bp2d", [1, E], F32, kind="ExternalInput").ap()
    bod = nc.dram_tensor("bod", [NCH, P], F32, kind="ExternalInput").ap()

    oT = nc.dram_tensor("oT", [NB, D, T], F32, kind="ExternalOutput").ap()
    x2sp = nc.dram_tensor("dbg_x2" if dbg else "x2sp", [NB, D, T], F32,
                          kind="ExternalOutput" if dbg else "Internal").ap()
    r3sp = nc.dram_tensor("dbg_r3" if dbg else "r3sp", [NB, 2, T], F32,
                          kind="ExternalOutput" if dbg else "Internal").ap()
    if dbg:
        dbg_y = nc.dram_tensor("dbg_y", [NB, D, T], F32,
                               kind="ExternalOutput").ap()
        dbg_rt = nc.dram_tensor("dbg_rt", [NB, 64], F32,
                                kind="ExternalOutput").ap()

    W1f = W1.rearrange("e k f -> (e k) f")
    W2f = W2.rearrange("e f h -> (e f) h")
    r3f = r3sp.rearrange("b k t -> (b k) t")

    with ExitStack() as top:
        def sb(name, shape, dtype):
            return top.enter_context(nc.sbuf_tensor(name, shape, dtype))

        def sem(name):
            return Sem(top, nc, name)

        ones128 = sb("ones128", [P, P], BF16)
        onesrow = sb("onesrow", [1, P], BF16)
        igb_sb = sb("igb_sb", [P, NCH], F32)
        agb_sb = sb("agb_sb", [P, NCH], F32)
        m8c = sb("m8c", [P, NCH], F32)
        wp2_sb = sb("wp2_sb", [P, E], BF16)
        bp1_sb = sb("bp1_sb", [P, 1], F32)
        bp1s_sb = sb("bp1s_sb", [P, 1], F32)
        bp2_sb = sb("bp2_sb", [1, E], F32)
        iotaW1 = sb("iotaW1", [P, NCH], F32)
        iotaW2 = sb("iotaW2", [P, NF], F32)
        iotaE1 = sb("iotaE1", [1, E], F32)
        iotaE2 = sb("iotaE2", [1, E], F32)
        zrow = sb("zrow", [1, E], F32)
        ones8t = sb("ones8t", [E, TB], BF16)
        ones1t = sb("ones1t", [1, TB], BF16)
        idxR3 = sb("idxR3", [P, 2 * NB], U32)
        Hcar = sb("Hcar", [P, NB * 2 * NCH], F32)
        ycar = sb("ycar", [P, NB * NCH], F32)
        x3lsb = sb("x3lsb", [P, NCH], F32)
        x2lsb = sb("x2lsb", [P, NCH], F32)
        w1h0 = sb("w1h0", [P, (NCH // 2) * FF], BF16)
        idnb = sb("idnb", [P, P], BF16)
        idnf = sb("idnf", [P, P], F32)
        x3lbf = sb("x3lbf", [P, NCH], BF16)
        gvec = sb("gvec", [P, 1], F32)
        evec = sb("evec", [P, 1], F32)
        wvec = sb("wvec", [P, 1], F32)
        bar1 = sb("bar1", [P, 1], F32)
        epsc = sb("epsc", [P, 1], F32)
        bar2 = sb("bar2", [P, 1], F32)
        u1bf = sb("u1bf", [P, 1], BF16)
        lgrow = sb("lgrow", [1, E], F32)
        mxr = sb("mxr", [1, 1], F32)
        ohrow = sb("ohrow", [1, E], F32)
        incl = sb("incl", [1, E], F32)
        fer = sb("fer", [1, E], F32)
        t8a = sb("t8a", [1, E], F32)
        t8b = sb("t8b", [1, E], F32)
        s1k = sb("s1k", [1, 1], F32)
        s2k = sb("s2k", [1, 1], F32)
        s1kb = sb("s1kb", [1, 1], BF16)
        s2kb = sb("s2kb", [1, 1], BF16)
        oh32 = sb("oh32", [32, 32], F32)
        ohT = sb("ohT", [32, 32], F32)
        selc = sb("selc", [P, 1], F32)
        idx1f = sb("idx1f", [P, NCH], F32)
        idx2f = sb("idx2f", [P, NF], F32)
        idx1u = sb("idx1u", [P, NCH * NB], U32)
        idx2u = sb("idx2u", [P, NF * NB], U32)
        ohcols = sb("ohcols", [E, NB], F32)

        LDP = sem("LDP")
        GPP = sem("GPP")
        SPR = sem("SPR")
        VPR = sem("VPR")
        LDX = sem("LDX")
        GXB = sem("GXB")
        S_SQ = sem("S_SQ")
        T_ST1 = sem("T_ST1")
        V_ST1 = sem("V_ST1")
        S_RI1 = sem("S_RI1")
        V_X1 = sem("V_X1")
        G_X1 = sem("G_X1")
        GX1B = sem("GX1B")
        T_G = sem("T_G")
        S_SIG = sem("S_SIG")
        V_LA = sem("V_LA")
        V_HS = sem("V_HS")
        V_D = sem("V_D")
        S_EXP = sem("S_EXP")
        V_M2 = sem("V_M2")
        S_MUL = sem("S_MUL")
        V_YS = sem("V_YS")
        V_CAR = sem("V_CAR")
        G_CAR = sem("G_CAR")
        GYB = sem("GYB")
        S_SQY = sem("S_SQY")
        T_ST2 = sem("T_ST2")
        V_ST2 = sem("V_ST2")
        S_RI2 = sem("S_RI2")
        V_AP2 = sem("V_AP2")
        G_MN2 = sem("G_MN2")
        V_X2 = sem("V_X2")
        SPL = sem("SPL")
        S_SQX2 = sem("S_SQX2")
        T_ST3 = sem("T_ST3")
        V_ST3 = sem("V_ST3")
        S_RI3 = sem("S_RI3")
        V_MN3 = sem("V_MN3")
        SPL3 = sem("SPL3")
        V_RT = sem("V_RT")
        GX3L = sem("GX3L")
        T_RT = sem("T_RT")
        S_RT = sem("S_RT")
        RTD = sem("RTD")
        BARR = sem("BARR")
        DBGY = sem("DBGY")
        GWLD = sem("GWLD")
        V_OHR = sem("V_OHR")
        T_BI = sem("T_BI")
        S_BI = sem("S_BI")
        ACC = sem("ACC")
        GW1 = sem("GW1")
        GW2 = sem("GW2")
        LB = sem("LB")
        GR3 = sem("GR3")
        V_X3M = sem("V_X3M")
        G_X3 = sem("G_X3")
        GX3B = sem("GX3B")
        T_H = sem("T_H")
        S_H = sem("S_H")
        T_R = sem("T_R")
        S_R = sem("S_R")
        T_O = sem("T_O")
        V_FF = sem("V_FF")
        G_FF = sem("G_FF")
        GFB = sem("GFB")
        S_SQ4 = sem("S_SQ4")
        T_S4 = sem("T_S4")
        V_ST4 = sem("V_ST4")
        S_RI4 = sem("S_RI4")
        V_FM = sem("V_FM")
        G_FM = sem("G_FM")
        V_FIN = sem("V_FIN")
        OUT = sem("OUT")

        block = top.enter_context(nc.Block())
        mst = {}

        def pro_sync(sync):
            for tsb, src in ((igb_sb, igb), (agb_sb, agb), (m8c, apar)):
                LDP.inc(sync.dma_start(out=tsb[:],
                                       in_=src.rearrange("c p -> p c"),
                                       allow_slow_non_contiguous=True), 16)
            LDP.inc(sync.dma_start(out=bp1_sb[:], in_=bp1d[:]), 16)
            LDP.inc(sync.dma_start(out=bp2_sb[:], in_=bp2d[:]), 16)
        block.sync(pro_sync)
        ldp_all = LDP.n

        def pro_gp(gpsimd):
            gpsimd.memset(ones128[:], 1.0)
            gpsimd.memset(onesrow[:], 1.0)
            gpsimd.memset(zrow[:], 0.0)
            gpsimd.memset(ones8t[:], 1.0)
            gpsimd.memset(epsc[:], EPS)
            gpsimd.memset(ones1t[:], 0.3)
            gpsimd.iota(idnf[:], pattern=[[1, P]], base=0,
                        channel_multiplier=-1,
                        allow_small_or_imprecise_dtypes=True)
            gpsimd.memset(oh32[:], 0.0)
            gpsimd.memset(Hcar[:], 0.0)
            gpsimd.memset(ycar[:], 0.0)
            gpsimd.iota(iotaW1[:], pattern=[[P, NCH]], base=0,
                        channel_multiplier=1,
                        allow_small_or_imprecise_dtypes=True)
            gpsimd.iota(iotaW2[:], pattern=[[P, NF]], base=0,
                        channel_multiplier=1,
                        allow_small_or_imprecise_dtypes=True)
            gpsimd.iota(iotaE1[:], pattern=[[D, E]], base=0,
                        channel_multiplier=0,
                        allow_small_or_imprecise_dtypes=True)
            gpsimd.iota(iotaE2[:], pattern=[[FF, E]], base=0,
                        channel_multiplier=0,
                        allow_small_or_imprecise_dtypes=True)
            GPP.inc(gpsimd.iota(idxR3[:], pattern=[[1, 2 * NB]], base=0,
                                channel_multiplier=0))
            GPP.inc(gpsimd.dma_start(out=wp2_sb[:], in_=Wp2[:]), 16)
        block.gpsimd(pro_gp)
        gpp_mem = 1

        def pro_scalar(scalar):
            ld_act(scalar, SET_LE)
            scalar.wait_ge(LDP.h, ldp_all)
            scalar.activation(out=m8c[:], in_=m8c[:], func=AF.Exp)
            SPR.inc(scalar.activation(out=m8c[:], in_=m8c[:], func=AF.Ln,
                                      bias=1.0))
        block.scalar(pro_scalar)

        def pro_vec(vector):
            vector.wait_ge(SPR.h, SPR.n)
            vector.wait_ge(GPP.h, 1)
            vector.tensor_scalar_mul(m8c[:], m8c[:], -8.0)
            VPR.inc(vector.tensor_single_scalar(out=idnf[:], in_=idnf[:],
                                                scalar=0.0, op=ALU.is_equal))
            vector.wait_ge(VPR.h, VPR.n)
            vector.tensor_copy(out=idnb[:], in_=idnf[:])
            VPR.inc(vector.tensor_scalar_mul(bp1s_sb[:], bp1_sb[:], RS2))
        block.vector(pro_vec)
        vpr_all = VPR.n

        with ExitStack() as actx:
            import dataclasses as _dc

            def asb(name, shape, dtype):
                return actx.enter_context(nc.sbuf_tensor(name, shape, dtype))

            def bcast_c(col, n=NCH, w=TA):
                ap = col[:, 0:w]
                return _dc.replace(ap, ap=[list(ap.ap[0]), [0, n]] +
                                   [list(d) for d in ap.ap[1:]])

            Ax = [asb(f"Axs{i}", [P, NCH * TA], F32) for i in range(2)]
            Axbf = asb("Axbf", [P, NCH * TA], BF16)
            Asq = [asb(f"Asq{i}", [P, NCH * TA], BF16) for i in range(3)]
            Ay = asb("Ay", [P, NCH * TA], F32)
            Ax1 = asb("Ax1", [P, NCH * TA], F32)
            Agx = asb("Agx", [P, NCH * TA], F32)
            Ala = asb("Ala", [P, NCH * TA], F32)
            AH = asb("AH", [P, NCH * TA], F32)
            Ax1bf = asb("Ax1bf", [P, NCH * TA], BF16)
            Lconst = asb("Lconst", [P, TA], F32)
            mean1 = [asb(f"mean1{i}", [P, TA], F32) for i in range(2)]
            vvb = asb("vvb", [P, TA], F32)
            ri1 = [asb(f"ri1{i}", [P, TA], F32) for i in range(2)]
            mn1 = [asb(f"mn1{i}", [P, TA], F32) for i in range(2)]
            ri2 = asb("ri2", [P, TA], F32)
            mn2 = asb("mn2", [P, TA], F32)
            ri3 = asb("ri3", [P, TA], F32)
            mn3 = asb("mn3", [P, TA], F32)
            psA = [actx.enter_context(nc.psum_tensor(f"psA{i}", [P, TA], F32))
                   for i in range(2)]
            psB = [actx.enter_context(nc.psum_tensor(f"psB{i}", [P, TA], F32))
                   for i in range(2)]
            psG = [actx.enter_context(nc.psum_tensor(f"psG{i}", [P, TA], F32))
                   for i in range(3)]
            psRL = actx.enter_context(nc.psum_tensor("psRL", [P, E], F32))
            psR = psRL[:, 0:1]
            psL = psRL[0:1, 0:E]

            igw_sb = asb("igw_sb", [P, NCH * P], BF16)
            agw_sb = asb("agw_sb", [P, NCH * P], BF16)
            wp1_sb = asb("wp1_sb", [P, NCH * P], BF16)

            def pro_gp2(gpsimd):
                gpsimd.memset(Lconst[:], LCL)
                GPP.inc(gpsimd.dma_start(
                    out=igw_sb[:].rearrange("p (h j) -> p h j", h=NCH),
                    in_=igw.rearrange("h i j -> i h j")), 16)
                GPP.inc(gpsimd.dma_start(
                    out=agw_sb[:].rearrange("p (h j) -> p h j", h=NCH),
                    in_=agw.rearrange("h i j -> i h j")), 16)
                GPP.inc(gpsimd.dma_start(
                    out=wp1_sb[:].rearrange("p (c j) -> p c j", c=NCH),
                    in_=Wp1.rearrange("(c p) j -> p c j", p=P)), 16)
            block.gpsimd(pro_gp2)
            gpp_dma = GPP.n
            gpp_lc = GPP.n

            def ch(buf, c, w=TA):
                return buf[:, c * w:(c + 1) * w]

            NG = 2
            GW = NCH // NG * TA

            def gr(buf, g):
                return buf[:, g * GW:(g + 1) * GW]

            ntiles = nb * nta

            def emit_prep(tau):
                b = tau % nb
                t0 = (tau // nb) * TA
                par = tau % 2
                xs = Ax[par]

                def p_load(sync, b=b, t0=t0, tau=tau, xs=xs):
                    if tau >= 2:
                        sync.wait_ge(V_X2.h, mst[("vx2", tau - 2)])
                    LDX.inc(sync.dma_start(
                        out=xs[:].rearrange("p (c t) -> p c t", c=NCH),
                        in_=xT[b, :, t0:t0 + TA].rearrange(
                            "(c p) t -> p c t", p=P)), 16)
                block.sync(p_load)
                mst[("ldx", tau)] = LDX.n

                def p_cast(gpsimd, tau=tau, xs=xs):
                    gpsimd.wait_ge(LDX.h, mst[("ldx", tau)])
                    if tau >= 1:
                        gpsimd.wait_ge(T_ST1.h, mst[("tst1", tau - 1)])
                    GXB.inc(gpsimd.dma_start(out=Axbf[:], in_=xs[:]), 16)
                block.gpsimd(p_cast)
                mst[("gxb", tau)] = GXB.n

                def p_sq(scalar, tau=tau, xs=xs):
                    scalar.wait_ge(LDX.h, mst[("ldx", tau)])
                    if tau >= 1:
                        scalar.wait_ge(T_ST1.h, mst[("tst1", tau - 1)])
                    S_SQ.inc(scalar.activation(out=Asq[0][:], in_=xs[:],
                                               func=AF.Square))
                block.scalar(p_sq)
                mst[("ssq", tau)] = S_SQ.n

                def p_st1(tensor, tau=tau, par=par):
                    if tau == 0:
                        tensor.wait_ge(GPP.h, gpp_mem)
                    tensor.wait_ge(GXB.h, mst[("gxb", tau)])
                    tensor.wait_ge(S_SQ.h, mst[("ssq", tau)])
                    if tau >= 2:
                        tensor.wait_ge(V_ST2.h, mst[("vst2", tau - 2)])
                    for c in range(NCH):
                        tensor.matmul(out=psA[par][:], lhsT=ones128[:],
                                      rhs=ch(Axbf, c), start=(c == 0),
                                      stop=(c == NCH - 1))
                    ins = None
                    for c in range(NCH):
                        ins = tensor.matmul(out=psB[par][:], lhsT=ones128[:],
                                            rhs=ch(Asq[0], c),
                                            start=(c == 0),
                                            stop=(c == NCH - 1))
                    T_ST1.inc(ins)
                block.tensor(p_st1)
                mst[("tst1", tau)] = T_ST1.n

            def emit_vst1(tau):
                par = tau % 2

                def p_vst1(vector, tau=tau, par=par):
                    vector.wait_ge(T_ST1.h, mst[("tst1", tau)])
                    vector.tensor_scalar(out=mean1[par][:], in0=psA[par][:],
                                         scalar1=1.0 / D, scalar2=None,
                                         op0=ALU.mult)
                    vector.scalar_tensor_tensor(out=vvb[:],
                                                in0=mean1[par][:],
                                                scalar=-1.0,
                                                in1=mean1[par][:],
                                                op0=ALU.mult, op1=ALU.mult)
                    V_ST1.inc(vector.scalar_tensor_tensor(
                        out=vvb[:], in0=psB[par][:], scalar=1.0 / D,
                        in1=vvb[:], op0=ALU.mult, op1=ALU.add))
                block.vector(p_vst1)
                mst[("vst1", tau)] = V_ST1.n

            def emit_ri1(tau):
                par = tau % 2

                def p_ri1(scalar, tau=tau, par=par):
                    scalar.wait_ge(V_ST1.h, mst[("vst1", tau)])
                    scalar.activation(out=ri1[par][:], in_=vvb[:],
                                      func=AF.Ln, bias=epsc[:, 0:1])
                    S_RI1.inc(scalar.activation(out=ri1[par][:],
                                                in_=ri1[par][:],
                                                func=AF.Exp, scale=-0.5))
                block.scalar(p_ri1)
                mst[("sri1", tau)] = S_RI1.n

            def emit_tail_vst2(tz):
                parz = tz % 2

                def a_vst2(vector, tz=tz, parz=parz):
                    vector.wait_ge(T_ST2.h, mst[("tst2", tz)])
                    vector.tensor_copy(out=mn1[parz][:], in_=mean1[parz][:])
                    vector.tensor_scalar(out=ri2[:], in0=psA[parz][:],
                                         scalar1=1.0 / D, scalar2=None,
                                         op0=ALU.mult)
                    vector.scalar_tensor_tensor(out=mn2[:], in0=ri2[:],
                                                scalar=-1.0, in1=ri2[:],
                                                op0=ALU.mult, op1=ALU.mult)
                    V_ST2.inc(vector.scalar_tensor_tensor(
                        out=mn2[:], in0=psB[parz][:], scalar=1.0 / D,
                        in1=mn2[:], op0=ALU.mult, op1=ALU.add))
                block.vector(a_vst2)
                mst[("vst2", tz)] = V_ST2.n

            def emit_tail_ri2(tz):
                def a_ri2(scalar, tz=tz):
                    scalar.wait_ge(V_ST2.h, mst[("vst2", tz)])
                    scalar.activation(out=mn2[:], in_=mn2[:], func=AF.Ln,
                                      bias=epsc[:, 0:1])
                    S_RI2.inc(scalar.activation(out=mn2[:], in_=mn2[:],
                                                func=AF.Exp, scale=-0.5))
                block.scalar(a_ri2)
                mst[("sri2", tz)] = S_RI2.n

            def emit_tail_apply(tz):
                bz = tz % nb
                tz0 = (tz // nb) * TA
                xsz = Ax[tz % 2]

                def a_ap2v(vector, tz=tz):
                    vector.wait_ge(S_RI2.h, mst[("sri2", tz)])
                    if dbg:
                        vector.wait_ge(DBGY.h, mst[("dbgy", tz)])
                    vector.scalar_tensor_tensor(out=ri2[:], in0=ri2[:],
                                                scalar=-1.0, in1=mn2[:],
                                                op0=ALU.mult, op1=ALU.mult)
                    for g in range(NG):
                        vector.tensor_tensor(
                            out=gr(Ay, g).rearrange("p (c t) -> p c t",
                                                    t=TA),
                            in0=gr(Ay, g).rearrange("p (c t) -> p c t",
                                                    t=TA),
                            in1=bcast_c(mn2, n=NCH // NG),
                            op=ALU.mult)
                        ins = vector.tensor_tensor(
                            out=gr(Ay, g).rearrange("p (c t) -> p c t",
                                                    t=TA),
                            in0=gr(Ay, g).rearrange("p (c t) -> p c t",
                                                    t=TA),
                            in1=bcast_c(ri2, n=NCH // NG),
                            op=ALU.add)
                        V_AP2.inc(ins)
                        mst[("vap2", tz, g)] = V_AP2.n
                block.vector(a_ap2v)

                def a_x2(vector, tz=tz, xsz=xsz):
                    ins = None
                    for g in range(NG):
                        ins = vector.tensor_add(gr(Ay, g), gr(Ay, g),
                                                gr(xsz, g))
                    V_X2.inc(ins)
                    vector.wait_ge(V_X2.h, V_X2.n)
                    V_X2.inc(vector.tensor_copy(
                        out=x2lsb[:, 0:NCH],
                        in_=Ay[:].rearrange("p (c t) -> p c t",
                                            c=NCH)[:, :, TA - 1]))
                block.vector(a_x2)
                mst[("vx2", tz)] = V_X2.n

                def a_spill(sync, bz=bz, tz0=tz0, tz=tz):
                    sync.wait_ge(V_X2.h, mst[("vx2", tz)])
                    SPL.inc(sync.dma_start(
                        out=x2sp[bz, :, tz0:tz0 + TA].rearrange(
                            "(c p) t -> p c t", p=P),
                        in_=Ay[:].rearrange("p (c t) -> p c t", c=NCH)),
                        16)
                block.sync(a_spill)
                mst[("spl", tz)] = SPL.n

            def emit_tail_sqx2(tz):
                parz = tz % 2

                def a_sqx2(scalar, tz=tz, parz=parz):
                    scalar.wait_ge(V_X2.h, mst[("vx2", tz)])
                    if tz >= 1:
                        scalar.wait_ge(T_ST3.h, mst[("tst3", tz - 1)])
                    S_SQX2.inc(scalar.activation(out=Asq[2][:],
                                                 in_=Ay[:],
                                                 func=AF.Square))
                block.scalar(a_sqx2)
                mst[("sqx2", tz)] = S_SQX2.n

            def emit_tail_st3(tz):
                parz = tz % 2

                def a_st3(tensor, tz=tz, parz=parz):
                    tensor.wait_ge(S_SQX2.h, mst[("sqx2", tz)])
                    tensor.wait_ge(V_ST2.h, mst[("vst2", tz)])
                    if tz + 2 < ntiles:
                        tensor.wait_ge(V_ST1.h, tz + 3)
                    ins = None
                    for c in range(NCH):
                        ins = tensor.matmul(out=psB[parz][:],
                                            lhsT=ones128[:],
                                            rhs=ch(Asq[2], c),
                                            start=(c == 0),
                                            stop=(c == NCH - 1))
                    T_ST3.inc(ins)
                block.tensor(a_st3)
                mst[("tst3", tz)] = T_ST3.n

            def emit_tail_ln3(tz):
                parz = tz % 2
                bz = tz % nb
                tz0 = (tz // nb) * TA

                def a_vst3(vector, tz=tz, parz=parz):
                    vector.wait_ge(T_ST3.h, mst[("tst3", tz)])
                    if tz > 0:
                        vector.wait_ge(SPL3.h, mst[("spl3", tz - 1)])
                    vector.scalar_tensor_tensor(out=ri3[:],
                                                in0=mn1[parz][:],
                                                scalar=-1.0,
                                                in1=mn1[parz][:],
                                                op0=ALU.mult, op1=ALU.mult)
                    V_ST3.inc(vector.scalar_tensor_tensor(
                        out=ri3[:], in0=psB[parz][:], scalar=1.0 / D,
                        in1=ri3[:], op0=ALU.mult, op1=ALU.add))
                block.vector(a_vst3)
                mst[("vst3", tz)] = V_ST3.n

                def a_ri3(scalar, tz=tz):
                    scalar.wait_ge(V_ST3.h, mst[("vst3", tz)])
                    scalar.activation(out=ri3[:], in_=ri3[:], func=AF.Ln,
                                      bias=epsc[:, 0:1])
                    S_RI3.inc(scalar.activation(out=ri3[:], in_=ri3[:],
                                                func=AF.Exp, scale=-0.5))
                block.scalar(a_ri3)
                mst[("sri3", tz)] = S_RI3.n

                def a_mn3(vector, tz=tz, parz=parz):
                    vector.wait_ge(S_RI3.h, mst[("sri3", tz)])
                    ins = vector.scalar_tensor_tensor(
                        out=mn3[:], in0=mn1[parz][:], scalar=-1.0,
                        in1=ri3[:], op0=ALU.mult, op1=ALU.mult)
                    V_MN3.inc(ins)
                block.vector(a_mn3)
                mst[("vmn3", tz)] = V_MN3.n

                def a_r3out(sync, bz=bz, tz0=tz0, tz=tz):
                    sync.wait_ge(V_MN3.h, mst[("vmn3", tz)])
                    SPL3.inc(sync.dma_start(out=r3sp[bz, 0:1, tz0:tz0 + TA],
                                            in_=ri3[0:1, :]), 16)
                    SPL3.inc(sync.dma_start(out=r3sp[bz, 1:2, tz0:tz0 + TA],
                                            in_=mn3[0:1, :]), 16)
                block.sync(a_r3out)
                mst[("spl3", tz)] = SPL3.n

            def emit_router(bz, tz):
                def rt_v1(vector, bz=bz, tz=tz):
                    vector.wait_ge(V_MN3.h, mst[("vmn3", tz)])
                    if bz > 0:
                        vector.wait_ge(GX3L.h, mst[("gx3l", bz - 1)])
                    ins = vector.tensor_tensor(
                        out=x3lsb[:, 0:NCH],
                        in0=x2lsb[:, 0:NCH],
                        in1=_dc.replace(ri3[:, TA - 1:TA],
                                        ap=[list(ri3[:, 0:1].ap[0]),
                                            [0, NCH]]),
                        op=ALU.mult)
                    V_RT.inc(ins)
                    vector.wait_ge(V_RT.h, V_RT.n)
                    ins = vector.tensor_tensor(
                        out=x3lsb[:, 0:NCH], in0=x3lsb[:, 0:NCH],
                        in1=_dc.replace(mn3[:, TA - 1:TA],
                                        ap=[list(mn3[:, 0:1].ap[0]),
                                            [0, NCH]]),
                        op=ALU.add)
                    V_RT.inc(ins)
                block.vector(rt_v1)
                mst[("vrt1", bz)] = V_RT.n

                def rt_g1(gpsimd, bz=bz):
                    gpsimd.wait_ge(V_RT.h, mst[("vrt1", bz)])
                    if bz > 0:
                        gpsimd.wait_ge(T_RT.h, mst[("trt1", bz - 1)])
                    GX3L.inc(gpsimd.dma_start(out=x3lbf[:], in_=x3lsb[:]),
                             16)
                block.gpsimd(rt_g1)
                mst[("gx3l", bz)] = GX3L.n

                def rt_t1(tensor, bz=bz):
                    tensor.wait_ge(GX3L.h, mst[("gx3l", bz)])
                    if bz > 0:
                        tensor.wait_ge(RTD.h, mst[("rtd", bz - 1)])
                    ins = None
                    for c in range(NCH):
                        ins = tensor.matmul(
                            out=psR, lhsT=wp1_sb[:, c * P:(c + 1) * P],
                            rhs=x3lbf[:, c:c + 1], start=(c == 0),
                            stop=(c == NCH - 1))
                    T_RT.inc(ins)
                block.tensor(rt_t1)
                mst[("trt1", bz)] = T_RT.n

                def rt_s1(scalar, bz=bz):
                    ld_act(scalar, SET_SG)
                    scalar.wait_ge(T_RT.h, mst[("trt1", bz)])
                    if bz == 0:
                        scalar.wait_ge(VPR.h, vpr_all)
                    scalar.activation(out=gvec[:], in_=psR,
                                      func=AF.Identity, bias=bp1_sb[:, 0:1])
                    ins = scalar.activation(out=evec[:], in_=psR,
                                            func=AF.Erf, scale=RS2,
                                            bias=bp1s_sb[:, 0:1])
                    S_RT.inc(ins)
                    ld_act(scalar, SET_LE)
                block.scalar(rt_s1)
                mst[("srt1", bz)] = S_RT.n

                def rt_v2(vector, bz=bz):
                    vector.wait_ge(S_RT.h, mst[("srt1", bz)])
                    V_RT.inc(vector.tensor_scalar(
                        out=wvec[:], in0=evec[:], scalar1=0.5, scalar2=0.5,
                        op0=ALU.mult, op1=ALU.add))
                    vector.wait_ge(V_RT.h, V_RT.n)
                    ins = vector.tensor_mul(u1bf[:], wvec[:], gvec[:])
                    V_RT.inc(ins)
                block.vector(rt_v2)
                mst[("vrt2", bz)] = V_RT.n

                def rt_t2(tensor, bz=bz):
                    tensor.wait_ge(V_RT.h, mst[("vrt2", bz)])
                    T_RT.inc(tensor.matmul(out=psL, lhsT=u1bf[:],
                                           rhs=wp2_sb[:], start=True,
                                           stop=True))
                block.tensor(rt_t2)
                mst[("trt2", bz)] = T_RT.n

                def rt_v3(vector, bz=bz):
                    vector.wait_ge(T_RT.h, mst[("trt2", bz)])
                    if bz == 0:
                        vector.wait_ge(LDP.h, ldp_all)
                    V_RT.inc(vector.tensor_add(lgrow[:], psL, bp2_sb[:]))
                    vector.wait_ge(V_RT.h, V_RT.n)
                    V_RT.inc(vector.reduce_max(mxr[:], lgrow[:], axis=AXX))
                    vector.wait_ge(V_RT.h, V_RT.n)
                    V_RT.inc(vector.tensor_scalar(
                        out=ohrow[:], in0=lgrow[:], scalar1=mxr[0:1, 0:1],
                        scalar2=None, op0=ALU.is_ge))
                    vector.wait_ge(V_RT.h, V_RT.n)
                    V_RT.inc(vector.tensor_tensor_scan(
                        out=incl[:], data0=ohrow[:], data1=zrow[:],
                        initial=0.0, op0=ALU.add, op1=ALU.add))
                    vector.wait_ge(V_RT.h, V_RT.n)
                    V_RT.inc(vector.tensor_single_scalar(
                        out=fer[:], in_=incl[:], scalar=1.0,
                        op=ALU.is_equal))
                    vector.wait_ge(V_RT.h, V_RT.n)
                    V_RT.inc(vector.tensor_mul(ohrow[:], ohrow[:], fer[:]))
                    vector.wait_ge(V_RT.h, V_RT.n)
                    vector.tensor_mul(t8a[:], ohrow[:], iotaE1[:])
                    vector.tensor_mul(t8b[:], ohrow[:], iotaE2[:])
                    V_RT.inc(vector.tensor_copy(out=oh32[0:1, 0:E],
                                                in_=ohrow[:]))
                    vector.wait_ge(V_RT.h, V_RT.n)
                    vector.reduce_sum(s1k[:], t8a[:], axis=AXX)
                    vector.reduce_sum(s2k[:], t8b[:], axis=AXX)
                    V_RT.inc(vector.transpose(ohT[:], oh32[:]))
                    vector.wait_ge(V_RT.h, V_RT.n)
                    vector.tensor_copy(out=s1kb[:], in_=s1k[:])
                    vector.tensor_copy(out=s2kb[:], in_=s2k[:])
                    ins = vector.tensor_copy(out=ohcols[:, bz:bz + 1],
                                             in_=ohT[0:E, 0:1])
                    V_RT.inc(ins)
                block.vector(rt_v3)
                mst[("vrt3", bz)] = V_RT.n

                def rt_t3(tensor, bz=bz):
                    tensor.wait_ge(V_RT.h, mst[("vrt3", bz)])
                    T_RT.inc(tensor.matmul(out=psR, lhsT=onesrow[:],
                                           rhs=s1kb[:], start=True,
                                           stop=True))
                block.tensor(rt_t3)
                mst[("trt3", bz)] = T_RT.n

                def rt_v4(vector, bz=bz):
                    vector.wait_ge(T_RT.h, mst[("trt3", bz)])
                    V_RT.inc(vector.tensor_copy(out=selc[:], in_=psR))
                    vector.wait_ge(V_RT.h, V_RT.n)
                    V_RT.inc(vector.tensor_scalar(
                        out=idx1f[:], in0=iotaW1[:], scalar1=selc[:, 0:1],
                        scalar2=None, op0=ALU.add))
                    vector.wait_ge(V_RT.h, V_RT.n)
                    ins = vector.tensor_copy(
                        out=idx1u[:, bz * NCH:(bz + 1) * NCH], in_=idx1f[:])
                    V_RT.inc(ins)
                block.vector(rt_v4)
                mst[("vrt4", bz)] = V_RT.n

                def rt_t4(tensor, bz=bz):
                    tensor.wait_ge(V_RT.h, mst[("vrt4", bz)])
                    T_RT.inc(tensor.matmul(out=psR, lhsT=onesrow[:],
                                           rhs=s2kb[:], start=True,
                                           stop=True))
                    if bz == nb - 1:
                        BARR.inc(tensor.matmul(
                            out=psL[0:1, 0:1], lhsT=onesrow[:, 0:1],
                            rhs=s2kb[:], start=True, stop=True))
                block.tensor(rt_t4)
                mst[("trt4", bz)] = T_RT.n

                def rt_v5(vector, bz=bz):
                    vector.wait_ge(T_RT.h, mst[("trt4", bz)])
                    V_RT.inc(vector.tensor_copy(out=selc[:], in_=psR))
                    vector.wait_ge(V_RT.h, V_RT.n)
                    V_RT.inc(vector.tensor_scalar(
                        out=idx2f[:], in0=iotaW2[:], scalar1=selc[:, 0:1],
                        scalar2=None, op0=ALU.add))
                    vector.wait_ge(V_RT.h, V_RT.n)
                    ins = vector.tensor_copy(
                        out=idx2u[:, bz * NF:(bz + 1) * NF], in_=idx2f[:])
                    RTD.inc(ins)
                block.vector(rt_v5)
                mst[("rtd", bz)] = RTD.n

                if dbg:
                    def rt_dbg(sync, bz=bz):
                        sync.wait_ge(RTD.h, mst[("rtd", bz)])
                        OUT.inc(sync.dma_start(
                            out=dbg_rt[bz:bz + 1, 0:E], in_=lgrow[:]), 16)
                        OUT.inc(sync.dma_start(
                            out=dbg_rt[bz:bz + 1, E:2 * E], in_=ohrow[:]),
                            16)
                    block.sync(rt_dbg)

            emit_prep(0)
            emit_vst1(0)
            emit_ri1(0)

            for tau in range(ntiles):
                b = tau % nb
                ti = tau // nb
                par = tau % 2
                t0 = ti * TA
                xs = Ax[par]

                def a_x1(vector, tau=tau, xs=xs, par=par):
                    vector.wait_ge(S_RI1.h, mst[("sri1", tau)])
                    if tau >= 1:
                        vector.wait_ge(GX1B.h, mst[("gx1b", tau - 1)])
                    vector.scalar_tensor_tensor(out=mn1[par][:],
                                                in0=mean1[par][:],
                                                scalar=-1.0,
                                                in1=ri1[par][:],
                                                op0=ALU.mult, op1=ALU.mult)
                    vector.tensor_tensor(
                        out=Ax1[:].rearrange("p (c t) -> p c t", c=NCH),
                        in0=xs[:].rearrange("p (c t) -> p c t", c=NCH),
                        in1=bcast_c(ri1[par]), op=ALU.mult)
                    ins = None
                    for g in range(NG):
                        ins = vector.tensor_tensor(
                            out=gr(Ax1, g).rearrange("p (c t) -> p c t",
                                                     t=TA),
                            in0=gr(Ax1, g).rearrange("p (c t) -> p c t",
                                                     t=TA),
                            in1=bcast_c(mn1[par], n=NCH // NG),
                            op=ALU.add)
                    V_X1.inc(ins)
                block.vector(a_x1)
                mst[("vx1", tau)] = V_X1.n

                if tau >= 1:
                    emit_tail_vst2(tau - 1)
                    emit_tail_ri2(tau - 1)

                def a_gx1(gpsimd, tau=tau):
                    gpsimd.wait_ge(V_X1.h, mst[("vx1", tau)])
                    if tau > 0:
                        gpsimd.wait_ge(T_ST2.h, mst[("tst2", tau - 1)])
                    GX1B.inc(gpsimd.dma_start(out=Ax1bf[:], in_=Ax1[:]), 16)
                block.gpsimd(a_gx1)
                mst[("gx1b", tau)] = GX1B.n

                ssig_base = S_SIG.n

                def a_gates(tensor, tau=tau, ssig_base=ssig_base):
                    if tau == 0:
                        tensor.wait_ge(GPP.h, gpp_dma)
                    tensor.wait_ge(GX1B.h, mst[("gx1b", tau)])
                    for c in range(NCH):
                        if c >= 1:
                            tensor.wait_ge(S_SIG.h, ssig_base + 2 * c - 2)
                        elif tau > 0:
                            tensor.wait_ge(S_SIG.h, ssig_base)
                        tensor.matmul(out=psG[(2 * c) % 3][:],
                                      lhsT=igw_sb[:, c * P:(c + 1) * P],
                                      rhs=ch(Ax1bf, c), start=True,
                                      stop=True)
                        if c >= 1:
                            tensor.wait_ge(S_SIG.h, ssig_base + 2 * c - 1)
                        ins = tensor.matmul(
                            out=psG[(2 * c + 1) % 3][:],
                            lhsT=agw_sb[:, c * P:(c + 1) * P],
                            rhs=ch(Ax1bf, c), start=True, stop=True)
                        T_G.inc(ins)
                        mst[("tg", tau, c)] = T_G.n
                block.tensor(a_gates)

                if tau >= 1:
                    emit_tail_apply(tau - 1)

                def a_sig(scalar, tau=tau):
                    ld_act(scalar, SET_SG)
                    if tau == 0:
                        scalar.wait_ge(LDP.h, ldp_all)
                    else:
                        scalar.wait_ge(V_YS.h, mst[("vys", tau - 1)])
                        scalar.wait_ge(V_D.h, mst[("vd", tau - 1)])
                    for c in range(NCH):
                        scalar.wait_ge(T_G.h, mst[("tg", tau, c)])
                        S_SIG.inc(scalar.activation(
                            out=ch(Agx, c), in_=psG[(2 * c) % 3][:],
                            func=AF.Sigmoid, bias=igb_sb[:, c:c + 1]))
                        ins = scalar.activation(out=ch(Ala, c),
                                                in_=psG[(2 * c + 1) % 3][:],
                                                func=AF.Sigmoid,
                                                bias=agb_sb[:, c:c + 1])
                        S_SIG.inc(ins)
                        mst[("ssig", tau, c)] = S_SIG.n
                    ld_act(scalar, SET_LE)
                    mst[("ssig_all", tau)] = S_SIG.n
                block.scalar(a_sig)

                if tau >= 1:
                    emit_tail_sqx2(tau - 1)

                def a_la(vector, tau=tau):
                    if tau == 0:
                        vector.wait_ge(VPR.h, vpr_all)
                    ins = None
                    for c in range(NCH):
                        vector.wait_ge(S_SIG.h, mst[("ssig", tau, c)])
                        ins = vector.tensor_scalar(
                            out=ch(Ala, c), in0=ch(Ala, c),
                            scalar1=m8c[:, c:c + 1], scalar2=None,
                            op0=ALU.mult)
                    V_LA.inc(ins)
                block.vector(a_la)
                mst[("vla", tau)] = V_LA.n

                def a_hs(vector, b=b, ti=ti, tau=tau):
                    if tau == 0:
                        vector.wait_ge(GPP.h, gpp_lc)
                    if tau > 0:
                        vector.wait_ge(S_MUL.h, mst[("smul", tau - 1)])
                    hb = b * 2 * NCH
                    ins = None
                    for c in range(NCH):
                        ins = vector.tensor_tensor_scan(
                            out=ch(AH, c), data0=ch(Ala, c),
                            data1=Lconst[:],
                            initial=Hcar[:, hb + 2 * c + 1:hb + 2 * c + 2],
                            op0=ALU.add, op1=ALU.max)
                    V_HS.inc(ins)
                block.vector(a_hs)
                mst[("vhs", tau)] = V_HS.n

                def a_d(vector, tau=tau, b=b):
                    if tau > 0:
                        vector.wait_ge(S_EXP.h, mst[("sexp", tau - 1)])
                        vector.wait_ge(SPL.h, mst[("spl", tau - 1)])
                        vector.wait_ge(S_SQX2.h, mst[("sqx2", tau - 1)])
                        vector.wait_ge(GYB.h, mst[("gyb", tau - 1)])
                        if dbg:
                            vector.wait_ge(DBGY.h, mst[("dbgy", tau - 1)])
                    Hv = AH[:].rearrange("p (c t) -> p c t", c=NCH)
                    dv = Ay[:].rearrange("p (c t) -> p c t", c=NCH)
                    Hodd = Hcar[:, b * 2 * NCH:(b + 1) * 2 * NCH].rearrange(
                        "p (c k) -> p c k", k=2)
                    vector.tensor_sub(dv[:, :, 0:1], Hodd[:, :, 1:2],
                                      Hodd[:, :, 0:1])
                    vector.tensor_sub(dv[:, :, 1:2], Hv[:, :, 0:1],
                                      Hodd[:, :, 1:2])
                    vector.tensor_sub(dv[:, :, 2:TA], Hv[:, :, 1:TA - 1],
                                      Hv[:, :, 0:TA - 2])
                    vector.tensor_copy(out=Hodd[:, :, 0:1],
                                       in_=Hv[:, :, TA - 2:TA - 1])
                    ins = vector.tensor_copy(out=Hodd[:, :, 1:2],
                                             in_=Hv[:, :, TA - 1:TA])
                    V_D.inc(ins)
                block.vector(a_d)
                mst[("vd", tau)] = V_D.n

                sexp_base = S_EXP.n

                def a_exp_a2(scalar, tau=tau):
                    scalar.wait_ge(V_HS.h, mst[("vhs", tau)])
                    for g in range(NG):
                        S_EXP.inc(scalar.activation(out=gr(Ala, g),
                                                    in_=gr(Ala, g),
                                                    func=AF.Exp, scale=2.0))
                block.scalar(a_exp_a2)

                def a_m2(vector, tau=tau, sexp_base=sexp_base):
                    for g in range(NG):
                        vector.wait_ge(S_EXP.h, sexp_base + g + 1)
                        vector.tensor_scalar(out=gr(AH, g), in0=gr(Ala, g),
                                             scalar1=-1.0, scalar2=1.0,
                                             op0=ALU.mult, op1=ALU.add)
                        ins = vector.tensor_scalar_max(gr(AH, g),
                                                       gr(AH, g), 1e-7)
                        V_M2.inc(ins)
                        mst[("vm2", tau, g)] = V_M2.n
                block.vector(a_m2)
                mst[("vm2", tau)] = V_M2.n

                def a_mul(scalar, tau=tau):
                    ins = None
                    for g in range(NG):
                        scalar.wait_ge(V_M2.h, mst[("vm2", tau, g)])
                        scalar.activation(out=gr(AH, g), in_=gr(AH, g),
                                          func=AF.Ln)
                        ins = scalar.activation(out=gr(AH, g), in_=gr(AH, g),
                                                func=AF.Exp, scale=0.5)
                        S_MUL.inc(ins)
                        mst[("smul", tau, g)] = S_MUL.n
                block.scalar(a_mul)
                mst[("smul", tau)] = S_MUL.n

                def a_exp_A(scalar, tau=tau):
                    scalar.wait_ge(V_D.h, mst[("vd", tau)])
                    for g in range(NG):
                        S_EXP.inc(scalar.activation(out=gr(Ay, g),
                                                    in_=gr(Ay, g),
                                                    func=AF.Exp))
                block.scalar(a_exp_A)
                mst[("sexp", tau)] = S_EXP.n

                def a_ys(vector, tau=tau, b=b):
                    for g in range(NG):
                        vector.wait_ge(S_MUL.h, mst[("smul", tau, g)])
                        vector.tensor_mul(gr(Agx, g), gr(Agx, g),
                                          gr(AH, g))
                        vector.tensor_mul(gr(Ax1, g), gr(Ax1, g),
                                          gr(Agx, g))
                    vector.wait_ge(S_EXP.h, mst[("sexp", tau)])
                    yb = b * NCH
                    ins = None
                    for c in range(NCH):
                        ins = vector.tensor_tensor_scan(
                            out=ch(Ay, c), data0=ch(Ay, c),
                            data1=ch(Ax1, c),
                            initial=ycar[:, yb + c:yb + c + 1], op0=ALU.mult,
                            op1=ALU.add)
                    V_YS.inc(ins)
                    mst[("vys", tau)] = V_YS.n
                    vector.wait_ge(V_YS.h, mst[("vys", tau)])
                    ins = vector.tensor_copy(
                        out=ycar[:, yb:yb + NCH],
                        in_=Ay[:].rearrange("p (c t) -> p c t",
                                            c=NCH)[:, :, TA - 1])
                    V_CAR.inc(ins)
                block.vector(a_ys)
                mst[("vcar", tau)] = V_CAR.n

                if tau + 1 < ntiles:
                    emit_prep(tau + 1)
                    emit_vst1(tau + 1)
                    emit_ri1(tau + 1)
                if tau >= 1:
                    emit_tail_st3(tau - 1)
                    emit_tail_ln3(tau - 1)
                    if (tau - 1) // nb == nta - 1:
                        emit_router((tau - 1) % nb, tau - 1)

                if dbg:
                    def a_dbgy(sync, b=b, t0=t0, tau=tau):
                        sync.wait_ge(V_YS.h, mst[("vys", tau)])
                        DBGY.inc(sync.dma_start(
                            out=dbg_y[b, :, t0:t0 + TA].rearrange(
                                "(c p) t -> p c t", p=P),
                            in_=Ay[:].rearrange("p (c t) -> p c t",
                                                c=NCH)), 16)
                    block.sync(a_dbgy)
                    mst[("dbgy", tau)] = DBGY.n

                def a_yb(gpsimd, tau=tau):
                    gpsimd.wait_ge(V_YS.h, mst[("vys", tau)])
                    if tau > 0:
                        gpsimd.wait_ge(T_ST2.h, mst[("tst2", tau - 1)])
                    GYB.inc(gpsimd.dma_start(out=Ax1bf[:], in_=Ay[:]), 16)
                block.gpsimd(a_yb)
                mst[("gyb", tau)] = GYB.n

                def a_sqy(scalar, tau=tau):
                    scalar.wait_ge(V_YS.h, mst[("vys", tau)])
                    if tau >= 1:
                        scalar.wait_ge(T_ST2.h, mst[("tst2", tau - 1)])
                    if dbg:
                        scalar.wait_ge(DBGY.h, mst[("dbgy", tau)])
                    S_SQY.inc(scalar.activation(out=Asq[1][:], in_=Ay[:],
                                                func=AF.Square))
                block.scalar(a_sqy)
                mst[("ssqy", tau)] = S_SQY.n

                def a_st2(tensor, tau=tau, par=par):
                    tensor.wait_ge(GYB.h, mst[("gyb", tau)])
                    tensor.wait_ge(S_SQY.h, mst[("ssqy", tau)])
                    tensor.wait_ge(V_ST1.h, mst[("vst1", tau)])
                    if tau >= 2:
                        tensor.wait_ge(V_ST3.h, mst[("vst3", tau - 2)])
                    for c in range(NCH):
                        tensor.matmul(out=psA[par][:], lhsT=ones128[:],
                                      rhs=ch(Ax1bf, c), start=(c == 0),
                                      stop=(c == NCH - 1))
                    ins = None
                    for c in range(NCH):
                        ins = tensor.matmul(out=psB[par][:],
                                            lhsT=ones128[:],
                                            rhs=ch(Asq[1], c),
                                            start=(c == 0),
                                            stop=(c == NCH - 1))
                    T_ST2.inc(ins)
                block.tensor(a_st2)
                mst[("tst2", tau)] = T_ST2.n

            def a_gw0(gpsimd):
                gpsimd.wait_ge(RTD.h, mst[("rtd", 0)])
                for kc in range(NCH // 2):
                    GW1.inc(gpsimd.indirect_dma_start(
                        out=w1h0[:, kc * FF:(kc + 1) * FF],
                        out_offset=None, in_=W1f,
                        in_offset=IOX(idx1u[:, kc:kc + 1], 0)), 16)
            block.gpsimd(a_gw0)
            mbseed_gw1h0_b0 = GW1.n

            emit_tail_vst2(ntiles - 1)
            emit_tail_ri2(ntiles - 1)
            emit_tail_apply(ntiles - 1)
            emit_tail_sqx2(ntiles - 1)
            emit_tail_st3(ntiles - 1)
            emit_tail_ln3(ntiles - 1)
            emit_router(nb - 1, ntiles - 1)

        def bar_v(vector):
            BARR.inc(vector.tensor_copy(out=bar1[:], in_=selc[:]))
        block.vector(bar_v)

        def bar_s(scalar):
            BARR.inc(scalar.copy(out=bar2[:], in_=selc[:]))
        block.scalar(bar_s)

        def bar_g(gpsimd):
            BARR.inc(gpsimd.memset(wvec[:], 0.0))
        block.gpsimd(bar_g)

        def bar_y(sync):
            BARR.inc(sync.dma_start(out=evec[:], in_=bp1_sb[:]), 16)
        block.sync(bar_y)
        barr_val = BARR.n

        if not do_b:
            def fin_a(sync):
                sync.wait_ge(SPL.h, SPL.n)
                sync.wait_ge(SPL3.h, SPL3.n)
                sync.wait_ge(BARR.h, barr_val)
                if OUT.n:
                    sync.wait_ge(OUT.h, OUT.n)
                if dbg:
                    sync.wait_ge(DBGY.h, DBGY.n)
            block.sync(fin_a)
            return nc

        with ExitStack() as bctx:
            import dataclasses as _dc2

            def bsb(name, shape, dtype):
                return bctx.enter_context(nc.sbuf_tensor(name, shape, dtype))

            w1h1 = bsb("w1h1", [P, (NCH // 2) * FF], BF16)
            w1h2 = bsb("w1h2", [P, (NCH // 2) * FF], BF16)
            b1all = bsb("b1all", [E, FF], BF16)
            b2all = bsb("b2all", [E, D], BF16)
            bo_r = bsb("bo_r", [1, D], BF16)
            w1slots = [w1h0, w1h1, w1h2]
            w2sb = bsb("w2sb", [P, NF * D], BF16)
            wo_sb = bsb("wo_sb", [P, NCH * D], BF16)
            x3B = bsb("x3B", [P, NCH * TB], F32)
            x3bf = [bsb(f"x3bf{i}", [P, NCH * TB], BF16) for i in range(2)]
            hB = bsb("hB", [P, NF * TB], BF16)
            rB = bsb("rB", [P, NCH * TB], BF16)
            fB = bsb("fB", [P, NCH * TB], F32)
            fbf = bsb("fbf", [P, NCH * TB], BF16)
            sqB = rB
            ohcb = bsb("ohcb", [E, 1], BF16)
            b1selc = bsb("b1selc", [P, NF], F32)
            b2selc = bsb("b2selc", [P, NCH], F32)
            R3t = bsb("R3t", [P, TB], F32)
            MN3t = bsb("MN3t", [P, TB], F32)
            mean4 = bsb("mean4", [P, TB], F32)
            vv4 = bsb("vv4", [P, TB], F32)
            ri4 = vv4
            ph = [bctx.enter_context(nc.psum_tensor(f"ph{i}", [P, TB], F32))
                  for i in range(2)]
            pr = [bctx.enter_context(nc.psum_tensor(f"pr{i}", [P, TB], F32))
                  for i in range(2)]
            po = [bctx.enter_context(nc.psum_tensor(f"po{i}", [P, TB], F32))
                  for i in range(2)]
            ps4 = bctx.enter_context(nc.psum_tensor("ps4", [P, TB], F32))
            psq4 = bctx.enter_context(nc.psum_tensor("psq4", [P, TB], F32))

            mb = {("gw1h0", 0): mbseed_gw1h0_b0}
            ntb = T // TB
            tbinfo = {tb: (tb // ntb, (tb % ntb) * TB)
                      for tb in range(nb * ntb)}

            def b_pro(gpsimd):
                gpsimd.wait_ge(BARR.h, barr_val)
                GWLD.inc(gpsimd.dma_start(
                    out=wo_sb[:].rearrange("p (c d) -> p c d", c=NCH),
                    in_=Wo.rearrange("(c p) d -> p c d", p=P)), 16)
                GWLD.inc(gpsimd.dma_start(out=b1all[:], in_=b1d[:]), 16)
                GWLD.inc(gpsimd.dma_start(out=b2all[:], in_=b2d[:]), 16)
                GWLD.inc(gpsimd.dma_start(
                    out=bo_r[:], in_=bod.rearrange("c p -> (c p)")[None, :]),
                    16)
            block.gpsimd(b_pro)
            gwld_all = GWLD.n

            def chB(buf, c, w=TB):
                return buf[:, c * w:(c + 1) * w]

            def w1slot(b, j):
                return (2 * b + j) % 3

            def emit_bprep(tb):
                b, t0 = tbinfo[tb]
                tc = t0 // TB

                def b_ld(sync, b=b, t0=t0, tb=tb):
                    if tb == 0:
                        sync.wait_ge(BARR.h, barr_val)
                    if tb > 0:
                        sync.wait_ge(GX3B.h, mb[("gx3b", tb - 1)])
                    sync.wait_ge(SPL.h, mst[("spl", (nta - 1) * nb + b)])
                    LB.inc(sync.dma_start(
                        out=x3B[:].rearrange("p (c t) -> p c t", c=NCH),
                        in_=x2sp[b, :, t0:t0 + TB].rearrange(
                            "(c p) t -> p c t", p=P)), 16)
                block.sync(b_ld)
                mb[("lb", tb)] = LB.n

                def b_r3(gpsimd, b=b, tc=tc, tb=tb):
                    if tb > 0:
                        gpsimd.wait_ge(V_X3M.h, mb[("vx3m", tb - 1)])
                    gpsimd.wait_ge(SPL3.h, mst[("spl3", (nta - 1) * nb + b)])
                    GR3.inc(gpsimd.indirect_dma_start(
                        out=R3t[:], out_offset=None, in_=r3f,
                        in_offset=IOX(idxR3[:, 2 * b:2 * b + 1], 0),
                        element_offset=tc * TB), 16)
                    GR3.inc(gpsimd.indirect_dma_start(
                        out=MN3t[:], out_offset=None, in_=r3f,
                        in_offset=IOX(idxR3[:, 2 * b + 1:2 * b + 2], 0),
                        element_offset=tc * TB), 16)
                block.gpsimd(b_r3)
                mb[("gr3", tb)] = GR3.n

                def b_x3(vector, tb=tb):
                    vector.wait_ge(LB.h, mb[("lb", tb)])
                    vector.wait_ge(GR3.h, mb[("gr3", tb)])
                    xv = x3B[:].rearrange("p (c t) -> p c t", c=NCH)
                    rb = R3t[:, 0:TB]
                    rb = _dc2.replace(rb, ap=[list(rb.ap[0]), [0, NCH]] +
                                      [list(d) for d in rb.ap[1:]])
                    mnb = MN3t[:, 0:TB]
                    mnb = _dc2.replace(mnb, ap=[list(mnb.ap[0]), [0, NCH]] +
                                       [list(d) for d in mnb.ap[1:]])
                    vector.tensor_tensor(out=xv, in0=xv, in1=rb, op=ALU.mult)
                    ins = vector.tensor_tensor(out=xv, in0=xv, in1=mnb,
                                               op=ALU.add)
                    V_X3M.inc(ins)
                block.vector(b_x3)
                mb[("vx3m", tb)] = V_X3M.n

                def b_x3g(gpsimd, tb=tb):
                    gpsimd.wait_ge(V_X3M.h, mb[("vx3m", tb)])
                    if tb > 1:
                        gpsimd.wait_ge(V_FF.h, mb[("vff_all", tb - 2)])
                    GX3B.inc(gpsimd.dma_start(out=x3bf[tb % 2][:],
                                              in_=x3B[:]), 16)
                block.gpsimd(b_x3g)
                mb[("gx3", tb)] = GX3B.n
                mb[("gx3b", tb)] = GX3B.n

            def emit_tail(tz):
                bz, tz0 = tbinfo[tz]

                def t_ffg(gpsimd, tz=tz):
                    gpsimd.wait_ge(V_FF.h, mb[("vff_all", tz)])
                    if tz > 0:
                        gpsimd.wait_ge(T_S4.h, mb[("ts4", tz - 1)])
                    GFB.inc(gpsimd.dma_start(out=fbf[:], in_=fB[:]), 16)
                block.gpsimd(t_ffg)
                mb[("gfb", tz)] = GFB.n

                def t_sq4(scalar, tz=tz):
                    scalar.wait_ge(V_FF.h, mb[("vff_all", tz)])
                    if tz > 0:
                        scalar.wait_ge(T_S4.h, mb[("ts4", tz - 1)])
                    S_SQ4.inc(scalar.activation(out=sqB[:], in_=fB[:],
                                                func=AF.Square))
                block.scalar(t_sq4)
                mb[("ssq4", tz)] = S_SQ4.n

                def t_st4(tensor, tz=tz):
                    tensor.wait_ge(GFB.h, mb[("gfb", tz)])
                    tensor.wait_ge(S_SQ4.h, mb[("ssq4", tz)])
                    if tz > 0:
                        tensor.wait_ge(V_ST4.h, mb[("vst4", tz - 1)])
                    for c in range(NCH):
                        tensor.matmul(out=ps4[:], lhsT=ones128[:],
                                      rhs=chB(fbf, c), start=(c == 0),
                                      stop=(c == NCH - 1))
                    ins = None
                    for c in range(NCH):
                        ins = tensor.matmul(out=psq4[:], lhsT=ones128[:],
                                            rhs=chB(sqB, c), start=(c == 0),
                                            stop=(c == NCH - 1))
                    T_S4.inc(ins)
                block.tensor(t_st4)
                mb[("ts4", tz)] = T_S4.n

                def t_vst4(vector, tz=tz):
                    vector.wait_ge(T_S4.h, mb[("ts4", tz)])
                    vector.tensor_scalar(out=mean4[:], in0=ps4[:],
                                         scalar1=1.0 / D, scalar2=None,
                                         op0=ALU.mult)
                    vector.scalar_tensor_tensor(out=vv4[:], in0=mean4[:],
                                                scalar=-1.0, in1=mean4[:],
                                                op0=ALU.mult, op1=ALU.mult)
                    V_ST4.inc(vector.scalar_tensor_tensor(
                        out=vv4[:], in0=psq4[:], scalar=1.0 / D, in1=vv4[:],
                        op0=ALU.mult, op1=ALU.add))
                block.vector(t_vst4)
                mb[("vst4", tz)] = V_ST4.n

                def t_ri4(scalar, tz=tz):
                    scalar.wait_ge(V_ST4.h, mb[("vst4", tz)])
                    scalar.activation(out=ri4[:], in_=vv4[:], func=AF.Ln,
                                      bias=epsc[:, 0:1])
                    S_RI4.inc(scalar.activation(out=ri4[:], in_=ri4[:],
                                                func=AF.Exp, scale=-0.5))
                block.scalar(t_ri4)
                mb[("sri4", tz)] = S_RI4.n

                def t_fin1(vector, tz=tz):
                    vector.wait_ge(S_RI4.h, mb[("sri4", tz)])
                    vector.scalar_tensor_tensor(out=mean4[:], in0=mean4[:],
                                                scalar=-1.0, in1=ri4[:],
                                                op0=ALU.mult, op1=ALU.mult)
                    rb = ri4[:, 0:TB]
                    rb = _dc2.replace(rb, ap=[list(rb.ap[0]), [0, NCH]] +
                                      [list(d) for d in rb.ap[1:]])
                    ins = vector.tensor_tensor(
                        out=fB[:].rearrange("p (c t) -> p c t", c=NCH),
                        in0=fB[:].rearrange("p (c t) -> p c t", c=NCH),
                        in1=rb, op=ALU.mult)
                    V_FM.inc(ins)
                    mb[("vfm", tz)] = V_FM.n
                block.vector(t_fin1)

                def t_fin2(gpsimd, tz=tz, bz=bz, tz0=tz0):
                    gpsimd.wait_ge(V_FM.h, mb[("vfm", tz)])
                    ins = None
                    for dc in range(NCH):
                        ins = gpsimd.tensor_add(chB(fB, dc), chB(fB, dc),
                                                mean4[:])
                    G_FM.inc(ins)
                    ACC.inc(gpsimd.dma_start(
                        out=fB[:].rearrange("p (c t) -> p c t", c=NCH),
                        in_=x2sp[bz, :, tz0:tz0 + TB].rearrange(
                            "(c p) t -> p c t", p=P),
                        accum_op=ALU.add), 16)
                block.gpsimd(t_fin2)
                mb[("acc", tz)] = ACC.n

                def t_out(sync, tz=tz, bz=bz, tz0=tz0):
                    sync.wait_ge(ACC.h, mb[("acc", tz)])
                    OUT.inc(sync.dma_start(
                        out=oT[bz, :, tz0:tz0 + TB].rearrange(
                            "(c p) t -> p c t", p=P),
                        in_=fB[:].rearrange("p (c t) -> p c t", c=NCH)), 16)
                block.sync(t_out)
                mb[("out", tz)] = OUT.n

            emit_bprep(0)

            for tb in range(nb * ntb):
                b, t0 = tbinfo[tb]
                tc = t0 // TB

                if tc == 0:
                    def b_gath(gpsimd, b=b):
                        gpsimd.wait_ge(RTD.h, mst[("rtd", b)])
                        gpsimd.wait_ge(BARR.h, barr_val)
                        if b > 0:
                            gpsimd.wait_ge(T_H.h, mb[("th", b * ntb - 1)])
                        s1 = w1slot(b, 1)
                        for kc in range(NCH // 2):
                            kg = NCH // 2 + kc
                            GW1.inc(gpsimd.indirect_dma_start(
                                out=w1slots[s1][:, kc * FF:(kc + 1) * FF],
                                out_offset=None, in_=W1f,
                                in_offset=IOX(
                                    idx1u[:, b * NCH + kg:b * NCH + kg + 1],
                                    0)), 16)
                        mb[("gw1h1", b)] = GW1.n
                        if b + 1 < nb:
                            gpsimd.wait_ge(RTD.h, mst[("rtd", b + 1)])
                            s0 = w1slot(b + 1, 0)
                            for kc in range(NCH // 2):
                                GW1.inc(gpsimd.indirect_dma_start(
                                    out=w1slots[s0][:,
                                                    kc * FF:(kc + 1) * FF],
                                    out_offset=None, in_=W1f,
                                    in_offset=IOX(
                                        idx1u[:, (b + 1) * NCH + kc:
                                              (b + 1) * NCH + kc + 1], 0)),
                                    16)
                        mb[("gw1h0", b + 1)] = GW1.n
                        if b > 0:
                            gpsimd.wait_ge(T_R.h, mb[("tr", b * ntb - 1)])
                        for fc in range(NF):
                            GW2.inc(gpsimd.indirect_dma_start(
                                out=w2sb[:, fc * D:(fc + 1) * D],
                                out_offset=None, in_=W2f,
                                in_offset=IOX(
                                    idx2u[:, b * NF + fc:b * NF + fc + 1],
                                    0)), 16)
                    block.gpsimd(b_gath)
                    mb[("gw2", b)] = GW2.n

                    def b_ohr(vector, b=b):
                        vector.wait_ge(RTD.h, mst[("rtd", b)])
                        if b > 0:
                            vector.wait_ge(T_BI.h, mb[("tbi", b - 1)])
                        else:
                            vector.wait_ge(BARR.h, barr_val)
                        V_OHR.inc(vector.tensor_copy(
                            out=ohcb[:], in_=ohcols[:, b:b + 1]))
                    block.vector(b_ohr)
                    mb[("vohr", b)] = V_OHR.n

                    def b_bias(tensor, b=b):
                        tensor.wait_ge(V_OHR.h, mb[("vohr", b)])
                        if b > 0:
                            tensor.wait_ge(S_H.h, mb[("sh_all", b * ntb - 1)])
                        else:
                            tensor.wait_ge(GWLD.h, gwld_all)
                        for fc in range(NF):
                            T_BI.inc(tensor.matmul(
                                out=ph[0][:, fc:fc + 1],
                                lhsT=b1all[:, fc * P:(fc + 1) * P],
                                rhs=ohcb[:], start=True, stop=True))
                        for hc in range(NCH):
                            T_BI.inc(tensor.matmul(
                                out=ph[0][:, NF + hc:NF + hc + 1],
                                lhsT=b2all[:, hc * P:(hc + 1) * P],
                                rhs=ohcb[:], start=True, stop=True))
                    block.tensor(b_bias)
                    mb[("tbi", b)] = T_BI.n

                    def b_bias_ev(scalar, b=b):
                        scalar.wait_ge(T_BI.h, mb[("tbi", b)])
                        scalar.activation(out=b1selc[:], in_=ph[0][:, 0:NF],
                                          func=AF.Identity)
                        ins = scalar.activation(out=b2selc[:],
                                                in_=ph[0][:, NF:NF + NCH],
                                                func=AF.Identity, scale=0.3)
                        S_BI.inc(ins)
                    block.scalar(b_bias_ev)
                    mb[("sbi", b)] = S_BI.n

                sh_base = S_H.n

                def b_w1(tensor, b=b, tb=tb, sh_base=sh_base):
                    tensor.wait_ge(GX3B.h, mb[("gx3b", tb)])
                    tensor.wait_ge(GW1.h, mb[("gw1h0", b)])
                    tensor.wait_ge(GW1.h, mb[("gw1h1", b)])
                    tensor.wait_ge(S_BI.h, mb[("sbi", b)])
                    for fc in range(NF):
                        if fc >= 2:
                            tensor.wait_ge(S_H.h, sh_base + fc - 1)
                        elif tb > 0:
                            tensor.wait_ge(S_H.h, sh_base)
                        sl = fc % 2
                        base = fc * P
                        ins = None
                        for kc in range(NCH):
                            slot = w1slots[w1slot(b, kc // (NCH // 2))]
                            kcl = kc % (NCH // 2)
                            ins = tensor.matmul(
                                out=ph[sl][:],
                                lhsT=slot[:, kcl * FF + base:
                                          kcl * FF + base + P],
                                rhs=chB(x3bf[tb % 2], kc), start=(kc == 0),
                                stop=(kc == NCH - 1))
                        T_H.inc(ins)
                        mb[("th", tb, fc)] = T_H.n
                    mb[("th", tb)] = T_H.n
                block.tensor(b_w1)

                def b_gelu(scalar, tb=tb, b=b):
                    ld_act(scalar, SET_GE)
                    for fc in range(NF):
                        scalar.wait_ge(T_H.h, mb[("th", tb, fc)])
                        ins = scalar.activation(out=chB(hB, fc),
                                                in_=ph[fc % 2][:],
                                                func=AF.Gelu,
                                                bias=b1selc[:, fc:fc + 1])
                        S_H.inc(ins)
                        mb[("sh", tb, fc)] = S_H.n
                    ld_act(scalar, SET_LE)
                    mb[("sh_all", tb)] = S_H.n
                block.scalar(b_gelu)

                if tb + 1 < nb * ntb:
                    emit_bprep(tb + 1)
                if tb > 0:
                    emit_tail(tb - 1)

                sr_base = S_R.n

                def b_w2(tensor, b=b, tb=tb, sr_base=sr_base):
                    tensor.wait_ge(S_H.h, mb[("sh_all", tb)])
                    tensor.wait_ge(GW2.h, mb[("gw2", b)])
                    ins = None
                    for hc in range(NCH):
                        if hc >= 2:
                            tensor.wait_ge(S_R.h, sr_base + hc - 1)
                        elif tb > 0:
                            tensor.wait_ge(S_R.h, sr_base)
                        sl = hc % 2
                        base = hc * P
                        ins = None
                        for fc in range(NF):
                            ins = tensor.matmul(
                                out=pr[sl][:],
                                lhsT=w2sb[:, fc * D + base:fc * D + base + P],
                                rhs=chB(hB, fc), start=(fc == 0),
                                stop=(fc == NF - 1))
                        T_R.inc(ins)
                        mb[("tr", tb, hc)] = T_R.n
                    mb[("tr", tb)] = T_R.n
                block.tensor(b_w2)

                def b_reas(scalar, tb=tb):
                    for hc in range(NCH):
                        scalar.wait_ge(T_R.h, mb[("tr", tb, hc)])
                        ins = scalar.activation(out=chB(rB, hc),
                                                in_=pr[hc % 2][:],
                                                func=AF.Identity, scale=0.3,
                                                bias=b2selc[:, hc:hc + 1])
                        S_R.inc(ins)
                        mb[("sr", tb, hc)] = S_R.n
                    mb[("sr_all", tb)] = S_R.n
                block.scalar(b_reas)

                vff_base = V_FF.n

                def b_wo(tensor, tb=tb, vff_base=vff_base):
                    tensor.wait_ge(S_R.h, mb[("sr_all", tb)])
                    for dc in range(NCH):
                        if dc >= 2:
                            tensor.wait_ge(V_FF.h, vff_base + dc - 1)
                        elif tb > 0:
                            tensor.wait_ge(V_FF.h, vff_base)
                        sl = dc % 2
                        for hc in range(NCH):
                            tensor.matmul(
                                out=po[sl][:],
                                lhsT=wo_sb[:, hc * D + dc * P:
                                           hc * D + dc * P + P],
                                rhs=chB(rB, hc), start=(hc == 0), stop=False)
                        ins = tensor.matmul(
                            out=po[sl][:], lhsT=bo_r[:, dc * P:(dc + 1) * P],
                            rhs=ones1t[:], start=False, stop=True)
                        T_O.inc(ins)
                        mb[("to", tb, dc)] = T_O.n
                block.tensor(b_wo)

                def b_ff(vector, tb=tb):
                    for dc in range(NCH):
                        vector.wait_ge(T_O.h, mb[("to", tb, dc)])
                        if tb > 0 and dc == 0:
                            vector.wait_ge(OUT.h, mb[("out", tb - 1)])
                        ins = vector.tensor_tensor(
                            out=chB(fB, dc), in0=po[dc % 2][:],
                            in1=chB(x3bf[tb % 2], dc), op=ALU.add)
                        V_FF.inc(ins)
                        mb[("vff", tb, dc)] = V_FF.n
                    mb[("vff_all", tb)] = V_FF.n
                block.vector(b_ff)

            emit_tail(nb * ntb - 1)

            def fin(sync):
                sync.wait_ge(OUT.h, OUT.n)
            block.sync(fin)

    return nc


def kernel(**inputs):
    x = np.asarray(inputs["L0_attn_x"], dtype=np.float32)
    W1 = np.ascontiguousarray(np.asarray(inputs["L0_ffn_W1"], dtype=np.float32))
    W2 = np.ascontiguousarray(np.asarray(inputs["L0_ffn_W2"], dtype=np.float32))
    Wo = np.ascontiguousarray(np.asarray(inputs["L0_ffn_Wo"], dtype=np.float32))
    Wp1 = np.ascontiguousarray(np.asarray(inputs["L0_ffn_Wp1"], dtype=np.float32))
    Wp2 = np.ascontiguousarray(np.asarray(inputs["L0_ffn_Wp2"], dtype=np.float32))
    b1 = np.ascontiguousarray(np.asarray(inputs["L0_ffn_b1"], dtype=np.float32))
    b2 = np.ascontiguousarray(np.asarray(inputs["L0_ffn_b2"], dtype=np.float32))
    bp1 = np.ascontiguousarray(
        np.asarray(inputs["L0_ffn_bp1"], dtype=np.float32).reshape(P, 1))
    bp2 = np.ascontiguousarray(
        np.asarray(inputs["L0_ffn_bp2"], dtype=np.float32).reshape(1, E))
    bo = np.ascontiguousarray(
        np.asarray(inputs["L0_ffn_bo"], dtype=np.float32).reshape(NCH, P))
    igw = np.ascontiguousarray(np.asarray(inputs["L0_attn_ig_w"], dtype=np.float32))
    agw = np.ascontiguousarray(np.asarray(inputs["L0_attn_ag_w"], dtype=np.float32))
    igb = np.ascontiguousarray(np.asarray(inputs["L0_attn_ig_b"], dtype=np.float32))
    agb = np.ascontiguousarray(np.asarray(inputs["L0_attn_ag_b"], dtype=np.float32))
    apar = np.ascontiguousarray(
        np.asarray(inputs["L0_attn_a_param"], dtype=np.float32).reshape(NCH, P))

    nc = build_nc()
    in_maps = []
    for i in range(8):
        xTc = np.ascontiguousarray(x[4 * i:4 * i + 4].transpose(0, 2, 1))
        in_maps.append(dict(xT=xTc, W1=W1, W2=W2, Wo=Wo, Wp1=Wp1, Wp2=Wp2,
                            b1d=b1, b2d=b2, igw=igw, agw=agw, igb=igb,
                            agb=agb, apar=apar, bp1d=bp1, bp2d=bp2, bod=bo))
    res = run_bass_kernel_spmd(nc, in_maps, core_ids=list(range(8)))
    out = np.concatenate(
        [np.asarray(r["oT"]).transpose(0, 2, 1) for r in res.results], axis=0)
    return np.ascontiguousarray(out.astype(np.float32))
